# version 02
# baseline (speedup 1.0000x reference)
import numpy as np
import concourse.bass as bass
import concourse.mybir as mybir

F32 = mybir.dt.float32
BF16 = mybir.dt.bfloat16
I32 = mybir.dt.int32
AF = mybir.ActivationFunctionType
ALU = mybir.AluOpType
AX = mybir.AxisListType

D = 2048
CL = 2048
CT = 16
NT = 16
DI = 4096
E0 = 8192
ET0 = 64
N1 = 16384
EPS = 1e-5
PAIRS = [[0, 1], [2, 3], [4, 5], [6, 7]]
ALL8 = [list(range(8))]


class Sem:
    def __init__(self, nc, name):
        self.cm = nc.semaphore(name)
        self.h = self.cm.__enter__()
        self.v = 0

    def inc(self, instr, n=1):
        instr.then_inc(self.h, n)
        self.v += n
        return self.v

    def wait(self, eng, v=None):
        eng.wait_ge(self.h, self.v if v is None else v)


class GatedBlock:

    def __init__(self, block, bar):
        self.block = block
        self.bar = bar
        self.level = 0

    def _wrap(self, attr):
        lvl = self.level

        def deco(fn):
            @attr
            def _(e):
                if lvl > 0:
                    e.wait_ge(self.bar.h, lvl)
                fn(e)
        return deco

    @property
    def tensor(self):
        return self._wrap(self.block.tensor)

    @property
    def vector(self):
        return self._wrap(self.block.vector)

    @property
    def scalar(self):
        return self._wrap(self.block.scalar)

    @property
    def gpsimd(self):
        return self._wrap(self.block.gpsimd)

    @property
    def sync(self):
        return self._wrap(self.block.sync)


def build_fused(upto=99):
    nc = bass.Bass("TRN2", target_bir_lowering=False)
    from contextlib import ExitStack

    x_own = nc.dram_tensor("x_own", [CL, D], F32, kind="ExternalInput").ap()
    x_sib = nc.dram_tensor("x_sib", [CL, D], F32, kind="ExternalInput").ap()
    W_up = nc.dram_tensor("W_up_s", [E0 // 8, D], F32, kind="ExternalInput").ap()
    W_gate = nc.dram_tensor("W_gate_s", [E0 // 8, D], F32, kind="ExternalInput").ap()
    W_down = nc.dram_tensor("W_down_s", [D // 8, E0], F32, kind="ExternalInput").ap()
    W_in = nc.dram_tensor("W_in", [D, 2 * DI], F32, kind="ExternalInput").ap()
    conv_w_in = nc.dram_tensor("conv_w_r", [128, 96], F32, kind="ExternalInput").ap()
    conv_b_in = nc.dram_tensor("conv_b_r", [128, 32], F32, kind="ExternalInput").ap()
    W_x = nc.dram_tensor("W_x", [DI, 160], F32, kind="ExternalInput").ap()
    W_dt = nc.dram_tensor("W_dt", [128, DI], F32, kind="ExternalInput").ap()
    b_dt_in = nc.dram_tensor("b_dt_r", [128, 32], F32, kind="ExternalInput").ap()
    A_in = nc.dram_tensor("A_r", [128, 512], F32, kind="ExternalInput").ap()
    D_in = nc.dram_tensor("D_r", [128, 32], F32, kind="ExternalInput").ap()
    W_out = nc.dram_tensor("W_out", [DI, D], F32, kind="ExternalInput").ap()
    gp_pk = nc.dram_tensor("gp_pk", [N1 // 8, 1024], I32, kind="ExternalInput").ap()
    gp_sc = nc.dram_tensor("gp_sc", [N1 // 8, 64], F32, kind="ExternalInput").ap()
    up_pk = nc.dram_tensor("up_pk", [N1 // 8, 1024], I32, kind="ExternalInput").ap()
    up_sc = nc.dram_tensor("up_sc", [N1 // 8, 64], F32, kind="ExternalInput").ap()
    dn_pk = nc.dram_tensor("dn_pk", [D // 8, E0], I32, kind="ExternalInput").ap()
    dn_sc = nc.dram_tensor("dn_sc", [D // 8, 512], F32, kind="ExternalInput").ap()
    eye_in = nc.dram_tensor("eye_bf", [128, 128], BF16, kind="ExternalInput").ap()
    maskh_in = nc.dram_tensor("maskh", [128, 1], F32, kind="ExternalInput").ap()
    maskl_in = nc.dram_tensor("maskl", [128, 2], F32, kind="ExternalInput").ap()
    thr_in = nc.dram_tensor("thrv", [1, 2], I32, kind="ExternalInput").ap()

    out_x4 = nc.dram_tensor("x4T", [D, CL], F32, kind="ExternalOutput").ap()

    x1T_d = nc.dram_tensor("x1T_d", [D, CL], BF16, kind="Internal").ap()
    x2T_d = nc.dram_tensor("x2T_d", [D, CL], BF16, kind="Internal").ap()
    x3T_d = nc.dram_tensor("x3T_d", [D, CL], BF16, kind="Internal").ap()
    xcT_d = nc.dram_tensor("xcT_d", [DI, CL], BF16, kind="Internal").ap()
    zT_d = nc.dram_tensor("zT_d", [DI, CL], BF16, kind="Internal").ap()
    dtT_d = nc.dram_tensor("dtT_d", [DI, CL], BF16, kind="Internal").ap()
    ysT_d = nc.dram_tensor("ysT_d", [DI, CL], BF16, kind="Internal").ap()
    bcd = nc.dram_tensor("bcd", [32, CL], F32, kind="Internal").ap()
    xbs = nc.dram_tensor("xbs", [CL, D], BF16, kind="Internal").ap()
    wib = nc.dram_tensor("wib", [ET0, NT, 128, 128], BF16, kind="Internal").ap()
    wob = nc.dram_tensor("wob", [NT, 32, 128, 128], BF16, kind="Internal").ap()
    l0u_l = nc.dram_tensor("l0u_l", [8, NT, 128, 128], BF16, kind="Internal").ap()
    l0g_l = nc.dram_tensor("l0g_l", [8, NT, 128, 128], BF16, kind="Internal").ap()
    l0d_l = nc.dram_tensor("l0d_l", [2, ET0, 128, 128], BF16, kind="Internal").ap()
    l0u_ag = nc.dram_tensor("l0u_ag", [8, 8, NT, 128, 128], BF16, kind="Internal").ap()
    l0g_ag = nc.dram_tensor("l0g_ag", [8, 8, NT, 128, 128], BF16, kind="Internal").ap()
    l0d_ag = nc.dram_tensor("l0d_ag", [8, 2, ET0, 128, 128], BF16, kind="Internal").ap()
    asum_loc = nc.dram_tensor("asum_loc", [1, 4], F32, kind="Internal").ap()
    asum_ag = nc.dram_tensor("asum_ag", [1, 4], F32, kind="Internal").ap()
    wgq_l = nc.dram_tensor("wgq_l", [NT, NT, 128, 128], BF16, kind="Internal").ap()
    wuq_l = nc.dram_tensor("wuq_l", [NT, NT, 128, 128], BF16, kind="Internal").ap()
    wdq_l = nc.dram_tensor("wdq_l", [128, 2, 128, 128], BF16, kind="Internal").ap()
    wgq_ag = nc.dram_tensor("wgq_ag", [8, NT, NT, 128, 128], BF16, kind="Internal").ap()
    wuq_ag = nc.dram_tensor("wuq_ag", [8, NT, NT, 128, 128], BF16, kind="Internal").ap()
    wdq_ag = nc.dram_tensor("wdq_ag", [8, 128, 2, 128, 128], BF16, kind="Internal").ap()
    hal_loc = nc.dram_tensor("hal_loc", [2, D], BF16, kind="Internal").ap()
    hal_ag = nc.dram_tensor("hal_ag", [4, D], BF16, kind="Internal").ap()
    hst_loc = nc.dram_tensor("hst_loc", [128, 512], F32, kind="Internal").ap()
    hst_ag = nc.dram_tensor("hst_ag", [256, 512], F32, kind="Internal").ap()

    def flat2(ap):
        return ap.rearrange("a b c d -> (a b c) d")

    def flat2r(ap):
        return ap.rearrange("r a b c d -> (r a b c) d")

    with ExitStack() as TOP:
        eye = TOP.enter_context(nc.sbuf_tensor("eye", [128, 128], BF16))
        ones_col = TOP.enter_context(nc.sbuf_tensor("ones_col", [128, 1], BF16))
        ones_colf = TOP.enter_context(nc.sbuf_tensor("ones_colf", [128, 1], F32))
        ones_row = TOP.enter_context(nc.sbuf_tensor("ones_row", [1, 128], F32))
        maskh = TOP.enter_context(nc.sbuf_tensor("maskh_sb", [128, 1], F32))
        maskl = TOP.enter_context(nc.sbuf_tensor("maskl_sb", [128, 2], F32))
        thrs = TOP.enter_context(nc.sbuf_tensor("thrs_sb", [1, 2], I32))
        rblock = TOP.enter_context(nc.Block())
        ln_smalls = {}

        def alloc_ln_smalls(scope, tag):
            ln_smalls["mu"] = scope.enter_context(nc.sbuf_tensor(f"mu{tag}", [1, 512], F32))
            ln_smalls["var"] = scope.enter_context(nc.sbuf_tensor(f"var{tag}", [1, 512], F32))
            ln_smalls["sdv"] = scope.enter_context(nc.sbuf_tensor(f"sdv{tag}", [1, 512], F32))
            ln_smalls["ivs"] = scope.enter_context(nc.sbuf_tensor(f"ivs{tag}", [1, 512], F32))
            ln_smalls["tnorm"] = scope.enter_context(nc.sbuf_tensor(f"tnorm{tag}", [128, 512], F32))
            ln_smalls["ysq"] = scope.enter_context(nc.sbuf_tensor(f"ysq{tag}", [128, 1024], BF16))

        bar = Sem(nc, "bar")
        block = GatedBlock(rblock, bar)
        q_s = Sem(nc, "q_s")
        q_g = Sem(nc, "q_g")
        q_o = Sem(nc, "q_o")
        s_pe = Sem(nc, "s_pe")
        s_dv = Sem(nc, "s_dv")
        s_sc = Sem(nc, "s_sc")
        s_ev = Sem(nc, "s_ev")
        s_aux = Sem(nc, "s_aux")
        s_cc = Sem(nc, "s_cc")
        s_sq = Sem(nc, "s_sq")
        s_st = Sem(nc, "s_st")
        s_nrm = Sem(nc, "s_nrm")
        s_raw = Sem(nc, "s_raw")
        s_p2 = Sem(nc, "s_p2")
        s_p3 = Sem(nc, "s_p3")
        s_p4 = Sem(nc, "s_p4")
        s_p5 = Sem(nc, "s_p5")

        def stage_barrier(deps):
            @rblock.sync
            def _(s):
                for h, v in deps:
                    s.wait_ge(h, v)
                ins = s.nop()
                bar.inc(ins)
            block.level += 1

        @block.sync
        def _(s):
            for dst, src in ((eye[:, :], eye_in[:, :]), (maskh[:, :], maskh_in[:, :]),
                             (maskl[:, :], maskl_in[:, :]), (thrs[:, :], thr_in[:, :])):
                ins = s.dma_start(out=dst, in_=src)
                q_s.inc(ins, 16)

        @block.gpsimd
        def _(g):
            g.memset(ones_col[:], 1.0)
            g.memset(ones_colf[:], 1.0)
            ins = g.memset(ones_row[:], 1.0)
            s_aux.inc(ins)
        init_q = q_s.v
        init_aux = s_aux.v

        ln_ctr = {"n": 0}

        def emit_ln_chunk(y_tile_fn, out_tile_fn, st_sum, st_sq, muB, ivB,
                          pre_wait=None, out_gate=None, first_deps=None):
            ci = ln_ctr["n"]
            ln_ctr["n"] += 1
            mu = ln_smalls["mu"]; var = ln_smalls["var"]; sdv = ln_smalls["sdv"]
            ivs = ln_smalls["ivs"]; tnorm = ln_smalls["tnorm"]; ysq = ln_smalls["ysq"]

            st_base = s_st.v

            @block.scalar
            def _(sc):
                for it in range(NT):
                    if it == 0 and pre_wait is not None:
                        pre_wait(sc)
                    if it == 0:
                        s_nrm.wait(sc, ci)
                    if it >= 2:
                        sc.wait_ge(s_st.h, st_base + it - 1)
                    ins = sc.activation(ysq[:, (it % 2) * 512:(it % 2) * 512 + 512],
                                        y_tile_fn(it), AF.Square)
                    s_sq.inc(ins)
            sq_base = s_sq.v - NT

            @block.tensor
            def _(t):
                if ci > 0:
                    s_nrm.wait(t, ci)
                t.wait_ge(s_aux.h, init_aux)
                if first_deps is not None:
                    for h, v in first_deps:
                        t.wait_ge(h, v)
                for it in range(NT):
                    t.wait_ge(s_sq.h, sq_base + it + 1)
                    t.matmul(st_sum[:, :], ones_col[:], y_tile_fn(it),
                             start=(it == 0), stop=(it == NT - 1))
                    last = t.matmul(st_sq[:, :], ones_col[:],
                                    ysq[:, (it % 2) * 512:(it % 2) * 512 + 512],
                                    start=(it == 0), stop=(it == NT - 1))
                    s_st.inc(last)

            @block.vector
            def _(v):
                s_st.wait(v)
                ins = v.tensor_scalar(mu[:, :], st_sum[:, :], 1.0 / D, None, ALU.mult)
                s_raw.inc(ins)
                v.tensor_scalar(var[:, :], st_sq[:, :], 1.0 / D, None, ALU.mult)
                s_raw.wait(v)
                ins = v.tensor_tensor(sdv[:, :], mu[:, :], mu[:, :], ALU.mult)
                s_raw.inc(ins)
                s_raw.wait(v)
                ins = v.tensor_tensor(var[:, :], var[:, :], sdv[:, :], ALU.subtract)
                s_raw.inc(ins)
                s_raw.wait(v)
                ins = v.tensor_scalar(var[:, :], var[:, :], EPS, None, ALU.add)
                s_aux.inc(ins)

            @block.scalar
            def _(sc):
                s_aux.wait(sc)
                ins = sc.activation(sdv[:, :], var[:, :], AF.Sqrt)
                s_aux.inc(ins)

            @block.vector
            def _(v):
                s_aux.wait(v)
                ins = v.reciprocal(ivs[:, :], sdv[:, :])
                s_aux.inc(ins)

            @block.tensor
            def _(t):
                s_aux.wait(t)
                t.matmul(muB[:, :], ones_row[:], mu[:, :], start=True, stop=True)
                ins = t.matmul(ivB[:, :], ones_row[:], ivs[:, :], start=True, stop=True)
                s_st.inc(ins)

            @block.vector
            def _(v):
                s_st.wait(v)
                if out_gate is not None:
                    out_gate(v)
                last = None
                for it in range(NT):
                    v.tensor_tensor(tnorm[:, :], y_tile_fn(it), muB[:, :], ALU.subtract)
                    last = v.tensor_tensor(out_tile_fn(it), tnorm[:, :], ivB[:, :], ALU.mult)
                s_nrm.inc(last)
            return ci

        L0BLOCKS = (
            [("u", W_up[i * 128:(i + 1) * 128, :], D, l0u_l[i], NT) for i in range(8)]
            + [("g", W_gate[i * 128:(i + 1) * 128, :], D, l0g_l[i], NT) for i in range(8)]
            + [("d", W_down[i * 128:(i + 1) * 128, :], E0, l0d_l[i], ET0) for i in range(2)]
        )
        MIDX = {"u": 0, "g": 1, "d": 2}
        NEL = float(E0 * D)

        with ExitStack() as S1:
            ldb0 = S1.enter_context(nc.sbuf_tensor("s1_ldb0", [128, E0], F32))
            ldb1 = S1.enter_context(nc.sbuf_tensor("s1_ldb1", [128, E0], F32))
            trash = S1.enter_context(nc.sbuf_tensor("s1_trash", [128, E0], BF16))
            acol = S1.enter_context(nc.sbuf_tensor("s1_acol", [128, 1], F32))
            wsum = S1.enter_context(nc.sbuf_tensor("s1_wsum", [128, 3], F32))
            prow = S1.enter_context(nc.sbuf_tensor("s1_prow", [1, 4], F32))
            grow = S1.enter_context(nc.sbuf_tensor("s1_grow", [1, 4], F32))
            trow = S1.enter_context(nc.sbuf_tensor("s1_trow", [1, 3], F32))
            ntrow = S1.enter_context(nc.sbuf_tensor("s1_ntrow", [1, 3], F32))
            nsrow = S1.enter_context(nc.sbuf_tensor("s1_nsrow", [1, 3], F32))
            thr_col = S1.enter_context(nc.sbuf_tensor("s1_thrc", [128, 3], F32))
            nthr_col = S1.enter_context(nc.sbuf_tensor("s1_nthrc", [128, 3], F32))
            nscl_col0 = S1.enter_context(nc.sbuf_tensor("s1_nsclc", [128, 3], F32))
            qpos = S1.enter_context(nc.sbuf_tensor("s1_qpos", [128, E0], BF16))
            qm0 = S1.enter_context(nc.sbuf_tensor("s1_qm0", [128, E0], BF16))
            qm1 = S1.enter_context(nc.sbuf_tensor("s1_qm1", [128, E0], BF16))
            tq0 = S1.enter_context(nc.sbuf_tensor("s1_tq0", [128, E0], BF16))
            tq1 = S1.enter_context(nc.sbuf_tensor("s1_tq1", [128, E0], BF16))
            sum_ps = S1.enter_context(nc.psum_tensor("s1_sum", [1, 3], F32))
            bc9 = S1.enter_context(nc.psum_tensor("s1_bc9", [128, 9], F32))
            tr_ps = S1.enter_context(nc.psum_tensor("s1_tr", [128, 2048], BF16))
            ldb = [ldb0, ldb1]
            qms = [qm0, qm1]
            tqs = [tq0, tq1]

            @block.gpsimd
            def _(g):
                ins = g.memset(wsum[:], 0.0)
                s_aux.inc(ins)
            ws_aux = s_aux.v

            ldm = {}
            for bi, (mn, src, w, _scr, _nt) in enumerate(L0BLOCKS):
                slot = bi % 2
                qq = q_s if bi % 2 == 0 else q_g
                eng = block.sync if bi % 2 == 0 else block.gpsimd

                @eng
                def _(e, bi=bi, slot=slot, src=src, w=w, qq=qq):
                    if bi >= 2:
                        s_sc.wait(e, bi - 1)
                    ins = e.dma_start(out=ldb[slot][:, 0:w], in_=src)
                    qq.inc(ins, 16)
                ldm[bi] = (q_s.v, q_g.v)

                @block.scalar
                def _(sc, bi=bi, slot=slot, w=w):
                    sc.wait_ge(q_s.h, ldm[bi][0])
                    sc.wait_ge(q_g.h, ldm[bi][1])
                    if bi >= 1:
                        s_dv.wait(sc, bi)
                    ins = sc.activation(trash[:, 0:w], ldb[slot][:, 0:w], AF.Abs,
                                        accum_out=acol[:, 0:1])
                    s_sc.inc(ins)

                @block.vector
                def _(v, bi=bi, mi=MIDX[mn]):
                    s_sc.wait(v, bi + 1)
                    if bi == 0:
                        v.wait_ge(s_aux.h, ws_aux)
                    ins = v.tensor_tensor(wsum[:, mi:mi + 1], wsum[:, mi:mi + 1],
                                          acol[:, 0:1], ALU.add)
                    s_dv.inc(ins)

            @block.tensor
            def _(t):
                s_dv.wait(t)
                ins = t.matmul(sum_ps[:, :], ones_colf[:], wsum[:, :], start=True, stop=True)
                s_pe.inc(ins)

            @block.vector
            def _(v):
                s_pe.wait(v)
                ins = v.tensor_copy(prow[:, 0:3], sum_ps[:, :])
                s_dv.inc(ins)
            pr_dv = s_dv.v

            @block.gpsimd
            def _(g):
                g.wait_ge(s_dv.h, pr_dv)
                ins = g.dma_start(out=asum_loc[:, :], in_=prow[:, :])
                q_g.inc(ins, 16)
                g.wait_ge(q_g.h, q_g.v)
                ins = g.collective_compute("AllReduce", ALU.add, replica_groups=ALL8,
                                           ins=[asum_loc.opt()], outs=[asum_ag.opt()])
                s_cc.inc(ins)
                g.wait_ge(s_cc.h, s_cc.v)
                ins = g.dma_start(out=grow[:, :], in_=asum_ag[:, :])
                q_g.inc(ins, 16)
            ar_done = q_g.v

            @block.vector
            def _(v):
                v.wait_ge(q_g.h, ar_done)
                v.tensor_scalar(trow[:, :], grow[:, 0:3], 0.5 / NEL, 0.5e-8, ALU.mult, ALU.add)
                v.tensor_scalar(ntrow[:, :], grow[:, 0:3], -0.5 / NEL, -0.5e-8, ALU.mult, ALU.add)
                ins = v.tensor_scalar(nsrow[:, :], grow[:, 0:3], -1.0 / NEL, -1e-8, ALU.mult, ALU.add)
                s_dv.inc(ins)

            @block.tensor
            def _(t):
                s_dv.wait(t)
                t.matmul(bc9[:, 0:3], ones_row[:], trow[:, :], start=True, stop=True)
                t.matmul(bc9[:, 3:6], ones_row[:], ntrow[:, :], start=True, stop=True)
                ins = t.matmul(bc9[:, 6:9], ones_row[:], nsrow[:, :], start=True, stop=True)
                s_pe.inc(ins)

            @block.vector
            def _(v):
                s_pe.wait(v)
                v.tensor_copy(thr_col[:, :], bc9[:, 0:3])
                v.tensor_copy(nthr_col[:, :], bc9[:, 3:6])
                ins = v.tensor_copy(nscl_col0[:, :], bc9[:, 6:9])
                s_dv.inc(ins)
            scl_ready = s_dv.v

            R0c = 0
            roundend = {}
            ldm2 = {}
            stm = {}
            for bi, (mn, src, w, scr, ntl_) in enumerate(L0BLOCKS):
                slot = bi % 2
                qq = q_s if bi % 2 == 0 else q_g
                eng = block.sync if bi % 2 == 0 else block.gpsimd
                ntiles = w // 128
                nrounds = ntiles // 8
                mi = MIDX[mn]

                @eng
                def _(e, bi=bi, slot=slot, src=src, w=w, qq=qq):
                    if bi < 2:
                        e.wait_ge(s_dv.h, scl_ready)
                    else:
                        s_p2.wait(e, bi - 1)
                    ins = e.dma_start(out=ldb[slot][:, 0:w], in_=src)
                    qq.inc(ins, 16)
                ldm2[bi] = (q_s.v, q_g.v)

                @block.vector
                def _(v, bi=bi, slot=slot, w=w, mi=mi):
                    v.wait_ge(q_s.h, ldm2[bi][0])
                    v.wait_ge(q_g.h, ldm2[bi][1])
                    if bi >= 2:
                        v.wait_ge(s_ev.h, roundend[bi - 2])
                    v.tensor_scalar(qpos[:, 0:w], ldb[slot][:, 0:w],
                                    thr_col[:, mi:mi + 1], None, ALU.is_ge)
                    ins = v.scalar_tensor_tensor(qms[slot][:, 0:w], ldb[slot][:, 0:w],
                                                 nthr_col[:, mi:mi + 1], qpos[:, 0:w],
                                                 ALU.is_le, ALU.subtract)
                    s_p2.inc(ins)

                @block.tensor
                def _(t, bi=bi, slot=slot, nrounds=nrounds, R0=R0c):
                    s_p2.wait(t, bi + 1)
                    if bi == 0:
                        t.wait_ge(q_s.h, init_q)
                    for rnd in range(nrounds):
                        Rg = R0 + rnd
                        if Rg >= 2:
                            s_ev.wait(t, Rg - 1)
                        half = (Rg % 2) * 1024
                        last = None
                        for k in range(8):
                            ti = rnd * 8 + k
                            last = t.transpose(tr_ps[:, half + k * 128: half + (k + 1) * 128],
                                               qms[slot][:, ti * 128:(ti + 1) * 128], eye[:])
                        s_pe.inc(last)
                pe_after = s_pe.v

                @block.vector
                def _(v, bi=bi, nrounds=nrounds, R0=R0c, tslot=bi % 2, pe_after=pe_after):
                    if bi >= 2:
                        v.wait_ge(q_o.h, stm[bi - 2])
                    for rnd in range(nrounds):
                        Rg = R0 + rnd
                        v.wait_ge(s_pe.h, pe_after - nrounds + rnd + 1)
                        half = (Rg % 2) * 1024
                        ins = v.tensor_copy(tqs[tslot][:, rnd * 1024:(rnd + 1) * 1024],
                                            tr_ps[:, half:half + 1024])
                        s_ev.inc(ins)

                @block.sync
                def _(s, bi=bi, scr=scr, tslot=bi % 2, ntiles=ntiles):
                    s_ev.wait(s)
                    ins = s.dma_start(
                        out=scr.rearrange("t p q -> p t q"),
                        in_=tqs[tslot][:, 0:ntiles * 128].rearrange("p (t q) -> p t q", t=ntiles))
                    q_o.inc(ins, 16)
                stm[bi] = q_o.v
                R0c += nrounds
                roundend[bi] = R0c

            @block.gpsimd
            def _(g):
                g.wait_ge(q_o.h, q_o.v)
                for src, dst in ((l0u_l, l0u_ag), (l0g_l, l0g_ag), (l0d_l, l0d_ag)):
                    ins = g.collective_compute("AllGather", ALU.bypass, replica_groups=ALL8,
                                               ins=[flat2(src).opt()],
                                               outs=[flat2r(dst).opt()])
                    s_cc.inc(ins)
        l0_cc = s_cc.v
        l0q_stores = q_o.v

        if upto <= 1:
            dbg = nc.dram_tensor("dbg_l0u", [8, 8, NT, 128, 128], BF16,
                                 kind="ExternalOutput").ap()
            dbg2 = nc.dram_tensor("dbg_l0d", [8, 2, ET0, 128, 128], BF16,
                                  kind="ExternalOutput").ap()

            @block.gpsimd
            def _(g):
                s_cc.wait(g)
                ins = g.dma_start(out=flat2r(dbg), in_=flat2r(l0u_ag))
                q_o.inc(ins, 16)
                ins = g.dma_start(out=flat2r(dbg2), in_=flat2r(l0d_ag))
                q_o.inc(ins, 16)
                g.wait_ge(q_o.h, q_o.v)
            return nc

        stage_barrier([(q_o.h, l0q_stores)])
        s2_gram = Sem(nc, "s2_gram")
        s2_mx = Sem(nc, "s2_mx")
        s2_exp = Sem(nc, "s2_exp")
        s2_tr = Sem(nc, "s2_tr")
        s2_ev = Sem(nc, "s2_ev")
        s2_y = Sem(nc, "s2_y")
        s2_q = Sem(nc, "s2_q")
        s2_g = Sem(nc, "s2_g")
        s2_aux = Sem(nc, "s2_aux")

        with ExitStack() as S2:
            ET = S2.enter_context(nc.sbuf_tensor("s2_ET", [128, NT * D], BF16))
            E_sb = S2.enter_context(nc.sbuf_tensor("s2_E", [128, D], BF16))
            rs = S2.enter_context(nc.sbuf_tensor("s2_rs", [128, NT], F32))
            invr = S2.enter_context(nc.sbuf_tensor("s2_invr", [128, NT], F32))
            mxs = S2.enter_context(nc.sbuf_tensor("s2_mx", [128, 1], F32))
            x1Tb = ET

            with ExitStack() as S2a:
                xb = S2a.enter_context(nc.sbuf_tensor("s2_xb", [128, 2 * CT * D], BF16))
                S_ps = S2a.enter_context(nc.psum_tensor("s2_S", [128, D], F32))
                tr2 = S2a.enter_context(nc.psum_tensor("s2_tr", [128, 2048], BF16))

                @block.gpsimd
                def _(g):
                    for src, base in ((x_own, 0), (x_sib, CT)):
                        for grp in range(2):
                            i0 = grp * 8
                            ins = g.dma_start(
                                out=xb[:, (base + i0) * D:(base + i0 + 8) * D]
                                    .rearrange("p (t d) -> p t d", t=8),
                                in_=src[i0 * 128:(i0 + 8) * 128, :]
                                    .rearrange("(t p) d -> p t d", p=128))
                            s2_g.inc(ins, 16)

                @block.sync
                def _(s):
                    s.wait_ge(s2_g.h, 32)
                    ins = s.dma_start(
                        out=xbs[:, :].rearrange("(t p) d -> p t d", p=128),
                        in_=xb[:, 0:CT * D].rearrange("p (t d) -> p t d", t=CT))
                    s2_q.inc(ins, 16)
                ld_all = s2_g.v

                def transposes2(t, it):
                    for rnd in range(2):
                        Rr = it * 2 + rnd
                        if Rr >= 2:
                            s2_ev.wait(t, Rr - 1)
                        half = (Rr % 2) * 1024
                        last = None
                        for k in range(8):
                            last = t.transpose(tr2[:, half + k * 128: half + (k + 1) * 128],
                                               E_sb[:, (rnd * 8 + k) * 128:(rnd * 8 + k + 1) * 128],
                                               eye[:])
                        s2_tr.inc(last)

                def evac2(v, it):
                    for rnd in range(2):
                        Rr = it * 2 + rnd
                        s2_tr.wait(v, Rr + 1)
                        half = (Rr % 2) * 1024
                        last = None
                        for k in range(8):
                            jt = rnd * 8 + k
                            last = v.tensor_copy(
                                ET[:, jt * D + it * 128: jt * D + (it + 1) * 128],
                                tr2[:, half + k * 128: half + (k + 1) * 128])
                        s2_ev.inc(last)

                for it in range(NT):
                    @block.tensor
                    def _(t, it=it):
                        if it == 0:
                            t.wait_ge(s2_g.h, ld_all)
                        else:
                            s2_exp.wait(t, it)
                        last = None
                        for ct in range(2 * CT):
                            for nchunk in range(4):
                                last = t.matmul(
                                    S_ps[:, nchunk * 512:(nchunk + 1) * 512],
                                    xb[:, ct * D + it * 128: ct * D + (it + 1) * 128],
                                    xb[:, ct * D + nchunk * 512: ct * D + (nchunk + 1) * 512],
                                    start=(ct == 0), stop=(ct == 2 * CT - 1))
                        s2_gram.inc(last)
                        if it > 0:
                            transposes2(t, it - 1)

                    @block.vector
                    def _(v, it=it):
                        s2_gram.wait(v, it + 1)
                        ins = v.tensor_reduce(mxs[:, :], S_ps[:, :], AX.X, ALU.max, negate=True)
                        s2_mx.inc(ins)

                    @block.scalar
                    def _(sc, it=it):
                        s2_mx.wait(sc, it + 1)
                        if it > 0:
                            s2_tr.wait(sc, 2 * it)
                        ins = sc.activation(E_sb[:, :], S_ps[:, :], AF.Exp,
                                            bias=mxs[:, 0:1], scale=1.0,
                                            accum_out=rs[:, it:it + 1])
                        s2_exp.inc(ins)

                    if it > 0:
                        @block.vector
                        def _(v, it=it):
                            evac2(v, it - 1)

                @block.tensor
                def _(t):
                    s2_exp.wait(t, NT)
                    transposes2(t, NT - 1)

                @block.vector
                def _(v):
                    evac2(v, NT - 1)
                    ins = v.reciprocal(invr[:, :], rs[:, :])
                    s2_aux.inc(ins)

            gram_done = s2_gram.v
            ev_done = s2_ev.v
            with ExitStack() as S2b:
                y2 = S2b.enter_context(nc.sbuf_tensor("s2_y", [128, NT * CL], BF16))
                _s2x = ExitStack()
                xT2 = _s2x.enter_context(nc.sbuf_tensor("s2_xT", [128, NT * CL], BF16))
                ctx_ps = _s2x.enter_context(nc.psum_tensor("s2_ctx", [128, CL], F32))

                @block.sync
                def _(s):
                    s.wait_ge(s2_q.h, 16)
                    s.wait_ge(s2_gram.h, gram_done)
                    for dt_ in range(NT):
                        ins = s.dma_start_transpose(xT2[:, dt_ * CL:(dt_ + 1) * CL],
                                                    xbs[:, dt_ * 128:(dt_ + 1) * 128])
                        s2_q.inc(ins, 16)
                tr_all = s2_q.v

                for it in range(NT):
                    @block.tensor
                    def _(t, it=it):
                        if it == 0:
                            t.wait_ge(s2_q.h, tr_all)
                            s2_aux.wait(t, 1)
                            s2_ev.wait(t, ev_done)
                        else:
                            s2_y.wait(t, it)
                        last = None
                        for jt in range(NT):
                            for nchunk in range(4):
                                last = t.matmul(
                                    ctx_ps[:, nchunk * 512:(nchunk + 1) * 512],
                                    ET[:, jt * D + it * 128: jt * D + (it + 1) * 128],
                                    xT2[:, jt * CL + nchunk * 512: jt * CL + (nchunk + 1) * 512],
                                    start=(jt == 0), stop=(jt == NT - 1))
                        s2_gram.inc(last)

                    @block.vector
                    def _(v, it=it):
                        s2_gram.wait(v, gram_done + it + 1)
                        ins = v.scalar_tensor_tensor(y2[:, it * CL:(it + 1) * CL],
                                                     ctx_ps[:, :], invr[:, it:it + 1],
                                                     xT2[:, it * CL:(it + 1) * CL],
                                                     ALU.mult, ALU.add)
                        s2_y.inc(ins)
                y_done2 = s2_y.v
                _s2x.close()

                with ExitStack() as S2c:
                    alloc_ln_smalls(S2c, "_s2")
                    st_sum = S2c.enter_context(nc.psum_tensor("s2_stsum", [1, 512], F32))
                    st_sq = S2c.enter_context(nc.psum_tensor("s2_stsq", [1, 512], F32))
                    muB = S2c.enter_context(nc.psum_tensor("s2_muB", [128, 512], F32))
                    ivB = S2c.enter_context(nc.psum_tensor("s2_ivB", [128, 512], F32))

                    for cchunk in range(4):
                        c0 = cchunk * 512
                        emit_ln_chunk(
                            (lambda it, c0=c0: y2[:, it * CL + c0: it * CL + c0 + 512]),
                            (lambda it, c0=c0: x1Tb[:, it * CL + c0: it * CL + c0 + 512]),
                            st_sum, st_sq, muB, ivB,
                            pre_wait=(lambda sc: sc.wait_ge(s2_y.h, y_done2))
                            if cchunk == 0 else None)

                        @block.sync
                        def _(s, c0=c0):
                            s_nrm.wait(s)
                            ins = s.dma_start(
                                out=x1T_d[:, c0:c0 + 512].rearrange("(g p) c -> p g c", p=128),
                                in_=x1Tb[:, :].rearrange("p (g c) -> p g c", g=NT)[:, :, c0:c0 + 512])
                            q_o.inc(ins, 16)

        s2_stores = q_o.v

        stage_barrier([(q_o.h, s2_stores)])
        d_qm = Sem(nc, "d_qm")
        d_tr = Sem(nc, "d_tr")
        d_ev = Sem(nc, "d_ev")
        d_rl = Sem(nc, "d_rl")
        d_q = Sem(nc, "d_q")
        d_g = Sem(nc, "d_g")
        d_o = Sem(nc, "d_o")
        d_aux = Sem(nc, "d_aux")

        DQBLOCKS = []
        for i in range(16):
            DQBLOCKS.append((gp_pk[i * 128:(i + 1) * 128, :], gp_sc[i * 128:(i + 1) * 128, :],
                             1024, wgq_l[:, i, :, :]))
        for i in range(16):
            DQBLOCKS.append((up_pk[i * 128:(i + 1) * 128, :], up_sc[i * 128:(i + 1) * 128, :],
                             1024, wuq_l[:, i, :, :]))
        for i in range(2):
            for sub in range(4):
                DQBLOCKS.append((dn_pk[i * 128:(i + 1) * 128, sub * 2048:(sub + 1) * 2048],
                                 dn_sc[i * 128:(i + 1) * 128, sub * 128:(sub + 1) * 128],
                                 2048, wdq_l[sub * 32:(sub + 1) * 32, i, :, :]))
        KQ = 2048

        with ExitStack() as SD:
            m4col = SD.enter_context(nc.sbuf_tensor("d_m4", [128, 1], F32))
            m6col = SD.enter_context(nc.sbuf_tensor("d_m6", [128, 1], F32))
            pk0 = SD.enter_context(nc.sbuf_tensor("d_pk0", [128, KQ], I32))
            pk1 = SD.enter_context(nc.sbuf_tensor("d_pk1", [128, KQ], I32))
            scb0 = SD.enter_context(nc.sbuf_tensor("d_sc0", [128, KQ // 16], F32))
            scb1 = SD.enter_context(nc.sbuf_tensor("d_sc1", [128, KQ // 16], F32))
            ni = SD.enter_context(nc.sbuf_tensor("d_ni", [128, KQ], I32))
            sgn = SD.enter_context(nc.sbuf_tensor("d_sgn", [128, KQ], BF16))
            mm_ = SD.enter_context(nc.sbuf_tensor("d_mm", [128, KQ], F32))
            r1 = SD.enter_context(nc.sbuf_tensor("d_r1", [128, KQ], BF16))
            r2 = SD.enter_context(nc.sbuf_tensor("d_r2", [128, KQ], BF16))
            t1q = SD.enter_context(nc.sbuf_tensor("d_t1", [128, KQ], F32))
            v2q = SD.enter_context(nc.sbuf_tensor("d_v2", [128, KQ], F32))
            sg2 = SD.enter_context(nc.sbuf_tensor("d_sg2", [128, KQ], BF16))
            qd0 = SD.enter_context(nc.sbuf_tensor("d_qd0", [128, 2 * KQ], BF16))
            qd1 = SD.enter_context(nc.sbuf_tensor("d_qd1", [128, 2 * KQ], BF16))
            tqd0 = SD.enter_context(nc.sbuf_tensor("d_tq0", [128, 2 * KQ], BF16))
            tqd1 = SD.enter_context(nc.sbuf_tensor("d_tq1", [128, 2 * KQ], BF16))
            trd = SD.enter_context(nc.psum_tensor("d_tr", [128, 2048], BF16))
            pks = [pk0, pk1]
            scbs = [scb0, scb1]
            qds = [qd0, qd1]
            tqds = [tqd0, tqd1]

            @block.gpsimd
            def _(g):
                g.memset(m4col[:], -4.0)
                ins = g.memset(m6col[:], -6.0)
                d_aux.inc(ins)

            Rq = 0
            d_roundend = {}
            d_ldm = {}
            d_stm = {}
            for bi, (pk_ap, sc_ap, Kp, out_ap) in enumerate(DQBLOCKS if upto > 2.05 else []):
                slot = bi % 2
                qq = d_q if bi % 2 == 0 else d_g
                eng = block.sync if bi % 2 == 0 else block.gpsimd
                ntiles = (2 * Kp) // 128
                nrounds = ntiles // 8

                @eng
                def _(e, bi=bi, slot=slot, Kp=Kp, pk_ap=pk_ap, sc_ap=sc_ap, qq=qq):
                    if bi >= 2 and upto > 2.1:
                        d_qm.wait(e, 2 * (bi - 1) if upto > 2.2 else (bi - 1))
                    ins = e.dma_start(out=pks[slot][:, 0:Kp], in_=pk_ap)
                    qq.inc(ins, 16)
                    ins = e.dma_start(out=scbs[slot][:, 0:Kp // 16], in_=sc_ap)
                    qq.inc(ins, 16)
                d_ldm[bi] = (d_q.v, d_g.v)

                if upto <= 2.1:
                    continue
                for pl in range(2):
                    if upto <= 2.15 and pl == 1:
                        continue

                    @block.vector
                    def _(v, bi=bi, slot=slot, Kp=Kp, pl=pl):
                        if pl == 0:
                            v.wait_ge(d_q.h, d_ldm[bi][0])
                            v.wait_ge(d_g.h, d_ldm[bi][1])
                            if bi == 0:
                                v.wait_ge(d_aux.h, 1)
                            if bi >= 2 and upto > 2.3:
                                v.wait_ge(d_tr.h, d_roundend[bi - 2])
                        if pl == 0:
                            v.tensor_scalar(ni[:, 0:Kp], pks[slot][:, 0:Kp], 15, None,
                                            ALU.bitwise_and)
                        else:
                            v.tensor_scalar(ni[:, 0:Kp], pks[slot][:, 0:Kp], 4, None,
                                            ALU.logical_shift_right)
                        v.tensor_scalar(sgn[:, 0:Kp], ni[:, 0:Kp], 8.0, None, ALU.is_ge)
                        ins = v.scalar_tensor_tensor(mm_[:, 0:Kp], sgn[:, 0:Kp], -8.0,
                                                     ni[:, 0:Kp], ALU.mult, ALU.add)
                        d_rl.inc(ins)

                    if upto <= 2.2:
                        continue

                    @block.scalar
                    def _(sc, bi=bi, Kp=Kp, pl=pl):
                        d_rl.wait(sc)
                        sc.activation(r1[:, 0:Kp], mm_[:, 0:Kp], AF.Relu, bias=m4col[:, 0:1])
                        ins = sc.activation(r2[:, 0:Kp], mm_[:, 0:Kp], AF.Relu, bias=m6col[:, 0:1])
                        d_rl.inc(ins)

                    if upto <= 2.2:
                        continue

                    @block.vector
                    def _(v, bi=bi, slot=slot, Kp=Kp, pl=pl):
                        d_rl.wait(v)
                        v.scalar_tensor_tensor(t1q[:, 0:Kp], r1[:, 0:Kp], 0.5, r2[:, 0:Kp],
                                               ALU.mult, ALU.add)
                        v.scalar_tensor_tensor(t1q[:, 0:Kp], mm_[:, 0:Kp], 0.5, t1q[:, 0:Kp],
                                               ALU.mult, ALU.add)
                        v.tensor_scalar(sg2[:, 0:Kp], sgn[:, 0:Kp], -2.0, 1.0, ALU.mult, ALU.add)
                        v.tensor_tensor(v2q[:, 0:Kp], t1q[:, 0:Kp], sg2[:, 0:Kp], ALU.mult)
                        Ks = Kp // 16
                        qv = qds[slot][:, 0:2 * Kp].rearrange("p (k j two) -> p k j two",
                                                              j=16, two=2)
                        ins = v.tensor_tensor(
                            qv[:, :, :, pl:pl + 1],
                            v2q[:, 0:Kp].rearrange("p (k j) -> p k j", j=16).unsqueeze(3),
                            scbs[slot][:, 0:Ks].unsqueeze(2).unsqueeze(3)
                                .broadcast_to([128, Ks, 16, 1]),
                            ALU.mult)
                        d_qm.inc(ins)

                if upto <= 2.3:
                    Rq += nrounds
                    d_roundend[bi] = Rq
                    d_stm[bi] = d_o.v
                    continue

                @block.tensor
                def _(t, bi=bi, slot=slot, ntiles=ntiles, nrounds=nrounds, R0=Rq):
                    d_qm.wait(t, 2 * (bi + 1))
                    for rnd in range(nrounds):
                        Rg = R0 + rnd
                        if Rg >= 2:
                            d_ev.wait(t, Rg - 1)
                        half = (Rg % 2) * 1024
                        last = None
                        for k in range(8):
                            ti = rnd * 8 + k
                            last = t.transpose(trd[:, half + k * 128: half + (k + 1) * 128],
                                               qds[slot][:, ti * 128:(ti + 1) * 128], eye[:])
                        d_tr.inc(last)

                @block.vector
                def _(v, bi=bi, nrounds=nrounds, R0=Rq, tslot=bi % 2):
                    if bi >= 2:
                        v.wait_ge(d_o.h, d_stm[bi - 2])
                    for rnd in range(nrounds):
                        Rg = R0 + rnd
                        d_tr.wait(v, Rg + 1)
                        half = (Rg % 2) * 1024
                        ins = v.tensor_copy(tqds[tslot][:, rnd * 1024:(rnd + 1) * 1024],
                                            trd[:, half:half + 1024])
                        d_ev.inc(ins)

                @block.sync
                def _(s, bi=bi, ntiles=ntiles, out_ap=out_ap, tslot=bi % 2):
                    d_ev.wait(s)
                    ins = s.dma_start(
                        out=out_ap.rearrange("t p q -> p t q"),
                        in_=tqds[tslot][:, 0:ntiles * 128].rearrange("p (t q) -> p t q", t=ntiles))
                    d_o.inc(ins, 16)
                d_stm[bi] = d_o.v
                Rq += nrounds
                d_roundend[bi] = Rq

            if upto > 2.4:
                @block.gpsimd
                def _(g):
                    g.wait_ge(d_o.h, d_o.v)
                    for src, dst in ((wgq_l, wgq_ag), (wuq_l, wuq_ag), (wdq_l, wdq_ag)):
                        ins = g.collective_compute("AllGather", ALU.bypass, replica_groups=ALL8,
                                                   ins=[flat2(src).opt()],
                                                   outs=[flat2r(dst).opt()])
                        s_cc.inc(ins)
        deq_cc = s_cc.v
        deq_stores = d_o.v

        if upto <= 2.5:
            dbgw = nc.dram_tensor("dbg_wgq", [NT, NT, 128, 128], BF16,
                                  kind="ExternalOutput").ap()

            @block.gpsimd
            def _(g):
                g.wait_ge(d_o.h, deq_stores)
                s_cc.wait(g)
                if upto <= 2.3:
                    g.wait_ge(d_qm.h, d_qm.v)
                if upto > 2.4:
                    ins = g.dma_start(out=flat2(dbgw),
                                      in_=flat2r(wgq_ag)[2 * 32768:3 * 32768, :])
                elif upto > 2.06:
                    ins = g.dma_start(out=flat2(dbgw), in_=flat2(wgq_l))
                else:
                    ins = g.dma_start(out=flat2(dbgw)[0:2048, :].rearrange("(g p) c -> p g c", p=128),
                                      in_=x1T_d[:, 0:128].rearrange("(g p) c -> p g c", p=128))
                q_o.inc(ins, 16)
                g.wait_ge(q_o.h, q_o.v)
            return nc

        stage_barrier([(d_o.h, deq_stores)])
        f_mm = Sem(nc, "f_mm")
        f_sil = Sem(nc, "f_sil")
        f_hid = Sem(nc, "f_hid")
        f_dn = Sem(nc, "f_dn")
        f_y = Sem(nc, "f_y")
        f_q = Sem(nc, "f_q")
        f_g2 = Sem(nc, "f_g2")
        f_o = Sem(nc, "f_o")
        f_hal = Sem(nc, "f_hal")

        with ExitStack() as S3:
            x1T = S3.enter_context(nc.sbuf_tensor("s3_x1T", [128, NT * CL], BF16))
            hid = S3.enter_context(nc.sbuf_tensor("s3_hid", [128, ET0 * 512], BF16))
            bu0 = S3.enter_context(nc.sbuf_tensor("s3_bu0", [128, NT * 128], BF16))
            bu1 = S3.enter_context(nc.sbuf_tensor("s3_bu1", [128, NT * 128], BF16))
            bg0 = S3.enter_context(nc.sbuf_tensor("s3_bg0", [128, NT * 128], BF16))
            bg1 = S3.enter_context(nc.sbuf_tensor("s3_bg1", [128, NT * 128], BF16))
            bd0 = S3.enter_context(nc.sbuf_tensor("s3_bd0", [128, ET0 * 128], BF16))
            silu_t = S3.enter_context(nc.sbuf_tensor("s3_silu", [128, 512], BF16))
            yb3 = S3.enter_context(nc.sbuf_tensor("s3_yb", [128, NT * 512], BF16))
            x2c = S3.enter_context(nc.sbuf_tensor("s3_x2c", [128, NT * 512], BF16))
            halL = S3.enter_context(nc.sbuf_tensor("s3_halL", [128, NT], BF16))
            halR = S3.enter_context(nc.sbuf_tensor("s3_halR", [128, NT], BF16))
            nscl_col = S3.enter_context(nc.sbuf_tensor("s3_nscl", [128, 3], F32))
            grow3 = S3.enter_context(nc.sbuf_tensor("s3_grow", [1, 4], F32))
            U_ps = S3.enter_context(nc.psum_tensor("s3_U", [128, 512], F32))
            G_ps = S3.enter_context(nc.psum_tensor("s3_G", [128, 512], F32))
            D_ps = S3.enter_context(nc.psum_tensor("s3_D", [128, 512], F32))
            bc3 = S3.enter_context(nc.psum_tensor("s3_bc3", [128, 3], F32))
            bus = [bu0, bu1]
            bgs = [bg0, bg1]

            @block.gpsimd
            def _(g):
                ins = g.dma_start(out=grow3[:, :], in_=asum_ag[:, :])
                f_g2.inc(ins, 16)

            @block.vector
            def _(v):
                v.wait_ge(f_g2.h, 16)
                ins = v.tensor_scalar(grow3[:, 0:3], grow3[:, 0:3], -1.0 / float(E0 * D),
                                      -1e-8, ALU.mult, ALU.add)
                f_hal.inc(ins)

            @block.tensor
            def _(t):
                f_hal.wait(t)
                ins = t.matmul(bc3[:, :], ones_row[:], grow3[:, 0:3], start=True, stop=True)
                f_mm.inc(ins)

            @block.vector
            def _(v):
                f_mm.wait(v)
                ins = v.tensor_copy(nscl_col[:, :], bc3[:, :])
                f_hal.inc(ins)
            nscl_rdy = f_hal.v

            @block.sync
            def _(s):
                ins = s.dma_start(
                    out=x1T[:, :].rearrange("p (g c) -> p g c", g=NT),
                    in_=x1T_d[:, :].rearrange("(g p) c -> p g c", p=128))
                f_q.inc(ins, 16)
            x1_ld = f_q.v

            upmark3 = {}
            dnmark3 = {}
            for cc in range(4):
                c0 = cc * 512
                for et in range(ET0):
                    ei = cc * ET0 + et
                    slot = ei % 2

                    @block.sync
                    def _(s, ei=ei, et=et, slot=slot):
                        if ei == 0:
                            s_cc.wait(s, l0_cc)
                        if ei >= 2:
                            f_hid.wait(s, ei - 1)
                        ins = s.dma_start(out=bus[slot][:, :].rearrange("p (t q) -> p t q", t=NT),
                                          in_=l0u_ag[et // 8, et % 8, :, :, :]
                                              .rearrange("t p q -> p t q"))
                        f_q.inc(ins, 16)
                        ins = s.dma_start(out=bgs[slot][:, :].rearrange("p (t q) -> p t q", t=NT),
                                          in_=l0g_ag[et // 8, et % 8, :, :, :]
                                              .rearrange("t p q -> p t q"))
                        f_q.inc(ins, 16)
                    upmark3[ei] = f_q.v

                    @block.tensor
                    def _(t, ei=ei, et=et, c0=c0, slot=slot):
                        t.wait_ge(f_q.h, upmark3[ei])
                        if ei == 0:
                            t.wait_ge(f_q.h, x1_ld)
                        if ei > 0:
                            f_hid.wait(t, ei)
                        last = None
                        for dt_ in range(NT):
                            t.matmul(U_ps[:, :], bus[slot][:, dt_ * 128:(dt_ + 1) * 128],
                                     x1T[:, dt_ * CL + c0: dt_ * CL + c0 + 512],
                                     start=(dt_ == 0), stop=(dt_ == NT - 1))
                            last = t.matmul(G_ps[:, :], bgs[slot][:, dt_ * 128:(dt_ + 1) * 128],
                                            x1T[:, dt_ * CL + c0: dt_ * CL + c0 + 512],
                                            start=(dt_ == 0), stop=(dt_ == NT - 1))
                        f_mm.inc(last)

                    @block.scalar
                    def _(sc, ei=ei):
                        f_mm.wait(sc)
                        if ei == 0:
                            sc.wait_ge(f_hal.h, nscl_rdy)
                        ins = sc.activation(silu_t[:, :], U_ps[:, :], AF.Silu,
                                            scale=nscl_col[:, 0:1])
                        f_sil.inc(ins)

                    @block.vector
                    def _(v, ei=ei, et=et):
                        f_sil.wait(v)
                        ins = v.scalar_tensor_tensor(hid[:, et * 512:(et + 1) * 512],
                                                     G_ps[:, :], nscl_col[:, 1:2],
                                                     silu_t[:, :], ALU.mult, ALU.mult)
                        f_hid.inc(ins)

                for d2t in range(NT):
                    di = cc * NT + d2t

                    @block.gpsimd
                    def _(g, di=di, d2t=d2t):
                        if di == 0:
                            g.wait_ge(s_cc.h, l0_cc)
                        if di >= 1:
                            f_dn.wait(g, di)
                        ins = g.dma_start(out=bd0[:, :].rearrange("p (t q) -> p t q", t=ET0),
                                          in_=l0d_ag[d2t // 2, d2t % 2, :, :, :]
                                              .rearrange("t p q -> p t q"))
                        f_g2.inc(ins, 16)
                    dnmark3[di] = f_g2.v

                    @block.tensor
                    def _(t, di=di, cc=cc):
                        t.wait_ge(f_g2.h, dnmark3[di])
                        f_hid.wait(t, (cc + 1) * ET0)
                        if di > 0:
                            f_y.wait(t, di)
                        last = None
                        for et in range(ET0):
                            last = t.matmul(D_ps[:, :], bd0[:, et * 128:(et + 1) * 128],
                                            hid[:, et * 512:(et + 1) * 512],
                                            start=(et == 0), stop=(et == ET0 - 1))
                        f_dn.inc(last)

                    @block.vector
                    def _(v, di=di, d2t=d2t, c0=c0):
                        f_dn.wait(v)
                        ins = v.scalar_tensor_tensor(yb3[:, d2t * 512:(d2t + 1) * 512],
                                                     D_ps[:, :], nscl_col[:, 2:3],
                                                     x1T[:, d2t * CL + c0: d2t * CL + c0 + 512],
                                                     ALU.mult, ALU.add)
                        f_y.inc(ins)
                y_done3 = f_y.v

                with ExitStack() as S3c:
                    alloc_ln_smalls(S3c, f"_s3_{cc}")
                    st3a = S3c.enter_context(nc.psum_tensor(f"s3_sts_{cc}", [1, 512], F32))
                    st3b = S3c.enter_context(nc.psum_tensor(f"s3_stq_{cc}", [1, 512], F32))
                    muB3 = S3c.enter_context(nc.psum_tensor(f"s3_muB_{cc}", [128, 512], F32))
                    ivB3 = S3c.enter_context(nc.psum_tensor(f"s3_ivB_{cc}", [128, 512], F32))
                    emit_ln_chunk(
                        (lambda it: yb3[:, it * 512:(it + 1) * 512]),
                        (lambda it: x2c[:, it * 512:(it + 1) * 512]),
                        st3a, st3b, muB3, ivB3,
                        pre_wait=(lambda sc, y_done3=y_done3: sc.wait_ge(f_y.h, y_done3)),
                        out_gate=(lambda v, cc=cc: v.wait_ge(f_o.h, 16 * cc)) if cc > 0 else None)

                    @block.sync
                    def _(s, cc=cc, c0=c0):
                        s_nrm.wait(s)
                        ins = s.dma_start(
                            out=x2T_d[:, c0:c0 + 512].rearrange("(g p) c -> p g c", p=128),
                            in_=x2c[:, :].rearrange("p (g c) -> p g c", g=NT))
                        f_o.inc(ins, 16)

                    if cc == 0 or cc == 3:
                        @block.vector
                        def _(v, cc=cc):
                            s_nrm.wait(v)
                            tgt = halL if cc == 0 else halR
                            colo = 0 if cc == 0 else 511
                            last = None
                            for it in range(NT):
                                last = v.tensor_copy(tgt[:, it:it + 1],
                                                     x2c[:, it * 512 + colo: it * 512 + colo + 1])
                            f_hal.inc(last)

            @block.sync
            def _(s):
                f_hal.wait(s)
                ins = s.dma_start(out=hal_loc[0:1, :].rearrange("one (p g) -> p (one g)", p=128),
                                  in_=halL[:, :])
                f_o.inc(ins, 16)
                ins = s.dma_start(out=hal_loc[1:2, :].rearrange("one (p g) -> p (one g)", p=128),
                                  in_=halR[:, :])
                f_o.inc(ins, 16)

            @block.gpsimd
            def _(g):
                g.wait_ge(f_o.h, f_o.v)
                ins = g.collective_compute("AllGather", ALU.bypass, replica_groups=PAIRS,
                                           ins=[hal_loc.opt()], outs=[hal_ag.opt()])
                s_cc.inc(ins)
        hal_cc = s_cc.v
        s3_stores = f_o.v

        stage_barrier([(f_o.h, s3_stores), (s_cc.h, hal_cc)])
        p_mm = Sem(nc, "p_mm")
        p_ev = Sem(nc, "p_ev")
        p_cv = Sem(nc, "p_cv")
        p_sil = Sem(nc, "p_sil")
        p_dt = Sem(nc, "p_dt")
        p_q = Sem(nc, "p_q")
        p_g = Sem(nc, "p_g")
        p_o = Sem(nc, "p_o")
        p_aux = Sem(nc, "p_aux")
        FT = 64

        with ExitStack() as S4:
            x2T4 = S4.enter_context(nc.sbuf_tensor("s4_x2T", [128, NT * CL], BF16))
            xh4 = S4.enter_context(nc.sbuf_tensor("s4_xh", [128, 32], BF16))
            halL2 = S4.enter_context(nc.sbuf_tensor("s4_hL", [128, NT], BF16))
            halR2 = S4.enter_context(nc.sbuf_tensor("s4_hR", [128, NT], BF16))
            wconv = S4.enter_context(nc.sbuf_tensor("s4_wcv", [128, 96], F32))
            bconv = S4.enter_context(nc.sbuf_tensor("s4_bcv", [128, 32], F32))
            bdt = S4.enter_context(nc.sbuf_tensor("s4_bdt", [128, 32], F32))
            wdt = S4.enter_context(nc.sbuf_tensor("s4_wdt", [128, DI], BF16))
            wx1 = S4.enter_context(nc.sbuf_tensor("s4_wx1", [128, 32 * 128], BF16))
            wx2 = S4.enter_context(nc.sbuf_tensor("s4_wx2", [128, 32 * 32], BF16))

            @block.sync
            def _(s):
                for dst, src in ((x2T4[:, :].rearrange("p (g c) -> p g c", g=NT),
                                  x2T_d[:, :].rearrange("(g p) c -> p g c", p=128)),
                                 (wconv[:, :], conv_w_in[:, :]),
                                 (bconv[:, :], conv_b_in[:, :]),
                                 (bdt[:, :], b_dt_in[:, :]),
                                 (halL2[:, :], hal_ag[1:2, :].rearrange("one (p g) -> p (one g)", p=128)),
                                 (halR2[:, :], hal_ag[2:3, :].rearrange("one (p g) -> p (one g)", p=128))):
                    ins = s.dma_start(out=dst, in_=src)
                    p_q.inc(ins, 16)

            @block.gpsimd
            def _(g):
                ins = g.dma_start(out=wdt[:, :], in_=W_dt[:, :])
                p_g.inc(ins, 16)
                ins = g.dma_start(
                    out=wx1[:, :].rearrange("p (g j) -> p g j", g=32),
                    in_=W_x[:, 0:128].rearrange("(g p) j -> p g j", p=128))
                p_g.inc(ins, 16)
                ins = g.dma_start(
                    out=wx2[:, :].rearrange("p (g j) -> p g j", g=32),
                    in_=W_x[:, 128:160].rearrange("(g p) j -> p g j", p=128))
                p_g.inc(ins, 16)
            pre_g = p_g.v
            pre_q = p_q.v

            @block.vector
            def _(v):
                v.wait_ge(p_q.h, pre_q)
                v.tensor_tensor(
                    xh4[:, :].rearrange("p (t two) -> p t two", two=2)[:, :, 0:1],
                    halL2[:, :].unsqueeze(2),
                    maskl[:, 0:1].unsqueeze(2).broadcast_to([128, NT, 1]), ALU.mult)
                ins = v.tensor_tensor(
                    xh4[:, :].rearrange("p (t two) -> p t two", two=2)[:, :, 1:2],
                    halR2[:, :].unsqueeze(2),
                    maskl[:, 1:2].unsqueeze(2).broadcast_to([128, NT, 1]), ALU.mult)
                p_aux.inc(ins)
            xh_rdy = p_aux.v

            with nc.sbuf_tensor("s4_wld0", [128, 2 * DI], BF16) as wld0, \
                 nc.sbuf_tensor("s4_wld1", [128, 2 * DI], BF16) as wld1:
                wld = [wld0, wld1]
                stmark4 = {}
                for hb in range(NT):
                    slot = hb % 2

                    @block.gpsimd
                    def _(g, hb=hb, slot=slot):
                        if hb >= 2:
                            g.wait_ge(p_q.h, stmark4[hb - 2])
                        ins = g.dma_start(out=wld[slot][:, :], in_=W_in[hb * 128:(hb + 1) * 128, :])
                        p_g.inc(ins, 16)
                    gmark = p_g.v

                    @block.sync
                    def _(s, hb=hb, slot=slot, gmark=gmark):
                        s.wait_ge(p_g.h, gmark)
                        ins = s.dma_start(
                            out=wib[:, hb, :, :].rearrange("f p q -> p f q"),
                            in_=wld[slot][:, :].rearrange("p (f q) -> p f q", f=FT))
                        p_q.inc(ins, 16)
                    stmark4[hb] = p_q.v
            wib_done = p_q.v

            with ExitStack() as S4a:
                wb0 = S4a.enter_context(nc.sbuf_tensor("s4_wb0", [128, NT * 128], BF16))
                wb1 = S4a.enter_context(nc.sbuf_tensor("s4_wb1", [128, NT * 128], BF16))
                xp_pad = S4a.enter_context(nc.sbuf_tensor("s4_xp", [128, CL + 2], F32))
                xcb0 = S4a.enter_context(nc.sbuf_tensor("s4_xc0", [128, CL], BF16))
                xcb1 = S4a.enter_context(nc.sbuf_tensor("s4_xc1", [128, CL], BF16))
                zb0 = S4a.enter_context(nc.sbuf_tensor("s4_zb0", [128, CL], BF16))
                zb1 = S4a.enter_context(nc.sbuf_tensor("s4_zb1", [128, CL], BF16))
                t14 = S4a.enter_context(nc.sbuf_tensor("s4_t1", [128, CL], F32))
                t24 = S4a.enter_context(nc.sbuf_tensor("s4_t2", [128, CL], F32))
                mm4 = S4a.enter_context(nc.psum_tensor("s4_mm", [128, 512], F32))
                halo4 = S4a.enter_context(nc.psum_tensor("s4_halo", [128, 2], F32))
                wbs4 = [wb0, wb1]
                xcbs = [xcb0, xcb1]
                zbs = [zb0, zb1]

                ldmark4 = {}
                outmark4 = {}
                for f in range(FT):
                    slot = f % 2
                    is_xp = f < 32

                    @block.sync
                    def _(s, f=f, slot=slot):
                        if f == 0:
                            s.wait_ge(p_q.h, wib_done)
                        if f >= 2:
                            p_mm.wait(s, p_mm.v - 1)
                        ins = s.dma_start(
                            out=wbs4[slot][:, :].rearrange("p (t q) -> p t q", t=NT),
                            in_=wib[f, :, :, :].rearrange("t p q -> p t q"))
                        p_q.inc(ins, 16)
                    ldmark4[f] = p_q.v

                    @block.tensor
                    def _(t, f=f, slot=slot, is_xp=is_xp):
                        t.wait_ge(p_q.h, ldmark4[f])
                        for nck in range(4):
                            if f > 0 or nck > 0:
                                p_ev.wait(t)
                            last = None
                            for ht in range(NT):
                                last = t.matmul(mm4[:, :], wbs4[slot][:, ht * 128:(ht + 1) * 128],
                                                x2T4[:, ht * CL + nck * 512: ht * CL + nck * 512 + 512],
                                                start=(ht == 0), stop=(ht == NT - 1))
                            p_mm.inc(last)

                            @block.vector
                            def _(v, f=f, nck=nck, is_xp=is_xp, slot=slot):
                                p_mm.wait(v)
                                if (not is_xp) and nck == 0 and f >= 2:
                                    v.wait_ge(p_o.h, outmark4[f - 2])
                                if is_xp:
                                    ins = v.tensor_copy(xp_pad[:, 1 + nck * 512: 1 + nck * 512 + 512],
                                                        mm4[:, :])
                                else:
                                    ins = v.tensor_copy(zbs[slot][:, nck * 512:(nck + 1) * 512],
                                                        mm4[:, :])
                                p_ev.inc(ins)
                        if is_xp:
                            t.wait_ge(p_aux.h, xh_rdy)
                            last = None
                            for ht in range(NT):
                                last = t.matmul(halo4[:, :], wbs4[slot][:, ht * 128:(ht + 1) * 128],
                                                xh4[:, ht * 2: ht * 2 + 2],
                                                start=(ht == 0), stop=(ht == NT - 1))
                            p_mm.inc(last)

                    if is_xp:
                        @block.vector
                        def _(v, f=f, slot=slot):
                            p_mm.wait(v)
                            v.tensor_copy(xp_pad[:, 0:1], halo4[:, 0:1])
                            ins = v.tensor_copy(xp_pad[:, CL + 1:CL + 2], halo4[:, 1:2])
                            p_ev.inc(ins)
                            v.tensor_scalar(t14[:, :], xp_pad[:, 0:CL], wconv[:, 3 * f: 3 * f + 1],
                                            None, ALU.mult)
                            v.scalar_tensor_tensor(t24[:, :], xp_pad[:, 1:CL + 1],
                                                   wconv[:, 3 * f + 1: 3 * f + 2], t14[:, :],
                                                   ALU.mult, ALU.add)
                            ins = v.scalar_tensor_tensor(t14[:, :], xp_pad[:, 2:CL + 2],
                                                         wconv[:, 3 * f + 2: 3 * f + 3], t24[:, :],
                                                         ALU.mult, ALU.add)
                            p_cv.inc(ins)

                        @block.scalar
                        def _(sc, f=f, slot=slot):
                            p_cv.wait(sc)
                            if f >= 2:
                                sc.wait_ge(p_o.h, outmark4[f - 2])
                            ins = sc.activation(xcbs[slot][:, :], t14[:, :], AF.Silu,
                                                bias=bconv[:, f:f + 1], scale=1.0)
                            p_sil.inc(ins)

                        @block.sync
                        def _(s, f=f, slot=slot):
                            p_sil.wait(s)
                            ins = s.dma_start(out=xcT_d[f * 128:(f + 1) * 128, :],
                                              in_=xcbs[slot][:, :])
                            p_o.inc(ins, 16)
                    else:
                        @block.sync
                        def _(s, f=f, slot=slot):
                            p_ev.wait(s)
                            ins = s.dma_start(out=zT_d[(f - 32) * 128:(f - 31) * 128, :],
                                              in_=zbs[slot][:, :])
                            p_o.inc(ins, 16)
                    outmark4[f] = p_o.v
            a_mm = p_mm.v
            a_out = p_o.v

            with ExitStack() as S4b:
                xcl0 = S4b.enter_context(nc.sbuf_tensor("s4_xcl0", [128, CL], BF16))
                xcl1 = S4b.enter_context(nc.sbuf_tensor("s4_xcl1", [128, CL], BF16))
                dtin = S4b.enter_context(nc.sbuf_tensor("s4_dtin", [128, CL], BF16))
                bc_sb = S4b.enter_context(nc.sbuf_tensor("s4_bc", [32, CL], F32))
                _s4p = ExitStack()
                P1 = _s4p.enter_context(nc.psum_tensor("s4_P1", [128, CL], F32))
                P2 = _s4p.enter_context(nc.psum_tensor("s4_P2", [32, CL], F32))
                xcls = [xcl0, xcl1]

                ldm4b = {}
                for g_ in range(32):
                    slot = g_ % 2

                    @block.gpsimd
                    def _(gg, g_=g_, slot=slot):
                        if g_ == 0:
                            gg.wait_ge(p_o.h, a_out)
                        if g_ >= 2:
                            p_mm.wait(gg, a_mm + g_ - 1)
                        ins = gg.dma_start(out=xcls[slot][:, :],
                                           in_=xcT_d[g_ * 128:(g_ + 1) * 128, :])
                        p_g.inc(ins, 16)
                    ldm4b[g_] = p_g.v

                    @block.tensor
                    def _(t, g_=g_, slot=slot):
                        t.wait_ge(p_g.h, ldm4b[g_])
                        if g_ == 0:
                            t.wait_ge(p_g.h, pre_g)
                        last = None
                        for ncc in range(4):
                            t.matmul(P1[:, ncc * 512:(ncc + 1) * 512],
                                     wx1[:, g_ * 128:(g_ + 1) * 128],
                                     xcls[slot][:, ncc * 512:(ncc + 1) * 512],
                                     start=(g_ == 0), stop=(g_ == 31))
                            last = t.matmul(P2[:, ncc * 512:(ncc + 1) * 512],
                                            wx2[:, g_ * 32:(g_ + 1) * 32],
                                            xcls[slot][:, ncc * 512:(ncc + 1) * 512],
                                            start=(g_ == 0), stop=(g_ == 31))
                        p_mm.inc(last)

                @block.vector
                def _(v):
                    p_mm.wait(v)
                    v.tensor_copy(dtin[:, :], P1[:, :])
                    ins = v.tensor_copy(bc_sb[:, :], P2[:, :])
                    p_ev.inc(ins)
                bce = p_ev.v

                @block.sync
                def _(s):
                    s.wait_ge(p_ev.h, bce)
                    ins = s.dma_start(out=bcd[:, :], in_=bc_sb[:, :])
                    p_o.inc(ins, 16)

                _s4p.close()
                with ExitStack() as S4c:
                    a1 = S4c.enter_context(nc.sbuf_tensor("s4_a1", [128, CL], F32))
                    lg = S4c.enter_context(nc.sbuf_tensor("s4_lg", [128, CL], F32))
                    rl = S4c.enter_context(nc.sbuf_tensor("s4_rl", [128, CL], F32))
                    dtb0 = S4c.enter_context(nc.sbuf_tensor("s4_dtb0", [128, CL], BF16))
                    dtb1 = S4c.enter_context(nc.sbuf_tensor("s4_dtb1", [128, CL], BF16))
                    dt_ps = S4c.enter_context(nc.psum_tensor("s4_dtps", [128, CL], F32))
                    dtbs = [dtb0, dtb1]
                    dtoutm = {}
                    for j in range(32):
                        slot = j % 2

                        @block.tensor
                        def _(t, j=j, slot=slot):
                            if j == 0:
                                t.wait_ge(p_ev.h, bce)
                            else:
                                p_dt.wait(t, p_dt.v)
                            last = None
                            for ncc in range(4):
                                last = t.matmul(dt_ps[:, ncc * 512:(ncc + 1) * 512],
                                                wdt[:, j * 128:(j + 1) * 128],
                                                dtin[:, ncc * 512:(ncc + 1) * 512],
                                                start=True, stop=True)
                            p_mm.inc(last)

                        @block.scalar
                        def _(sc, j=j, slot=slot):
                            p_mm.wait(sc)
                            sc.activation(a1[:, :], dt_ps[:, :], AF.Abs, bias=bdt[:, j:j + 1],
                                          scale=1.0)
                            sc.activation(lg[:, :], a1[:, :], AF.Exp, scale=-1.0)
                            ins = sc.activation(lg[:, :], lg[:, :], AF.Ln, bias=1.0, scale=1.0)
                            p_aux.inc(ins)

                        @block.vector
                        def _(v, j=j, slot=slot):
                            p_aux.wait(v)
                            if j >= 2:
                                v.wait_ge(p_o.h, dtoutm[j - 2])
                            v.tensor_scalar(rl[:, :], dt_ps[:, :], bdt[:, j:j + 1], 0.0,
                                            ALU.add, ALU.max)
                            ins = v.tensor_tensor(dtbs[slot][:, :], rl[:, :], lg[:, :], ALU.add)
                            p_dt.inc(ins)

                        @block.sync
                        def _(s, j=j, slot=slot):
                            p_dt.wait(s)
                            ins = s.dma_start(out=dtT_d[j * 128:(j + 1) * 128, :],
                                              in_=dtbs[slot][:, :])
                            p_o.inc(ins, 16)
                        dtoutm[j] = p_o.v
        s4_out = p_o.v

        stage_barrier([(p_o.h, s4_out)])
        c_a = Sem(nc, "c_a")
        c_dv = Sem(nc, "c_dv")
        c_q = Sem(nc, "c_q")
        c_g = Sem(nc, "c_g")
        c_o = Sem(nc, "c_o")
        c_aux = Sem(nc, "c_aux")
        c_hend = Sem(nc, "c_hend")
        c_hld = Sem(nc, "c_hld")
        TSC = 2048

        with ExitStack() as S5:
            Bb5 = S5.enter_context(nc.sbuf_tensor("s5_Bb", [128, 16 * TSC], BF16))
            Cb5 = S5.enter_context(nc.sbuf_tensor("s5_Cb", [128, 16 * TSC], BF16))
            dtg0 = S5.enter_context(nc.sbuf_tensor("s5_dt0", [128, TSC], BF16))
            dtg1 = S5.enter_context(nc.sbuf_tensor("s5_dt1", [128, TSC], BF16))
            xcg0 = S5.enter_context(nc.sbuf_tensor("s5_xc0", [128, TSC], BF16))
            xcg1 = S5.enter_context(nc.sbuf_tensor("s5_xc1", [128, TSC], BF16))
            dug5 = S5.enter_context(nc.sbuf_tensor("s5_du", [128, TSC], BF16))
            a50 = S5.enter_context(nc.sbuf_tensor("s5_a0", [128, TSC], BF16))
            a51 = S5.enter_context(nc.sbuf_tensor("s5_a1", [128, TSC], BF16))
            w5 = S5.enter_context(nc.sbuf_tensor("s5_w", [128, TSC], BF16))
            h5 = S5.enter_context(nc.sbuf_tensor("s5_h", [128, TSC], BF16))
            hc5 = S5.enter_context(nc.sbuf_tensor("s5_hc", [128, TSC], BF16))
            y50 = S5.enter_context(nc.sbuf_tensor("s5_y0", [128, TSC], BF16))
            y51 = S5.enter_context(nc.sbuf_tensor("s5_y1", [128, TSC], BF16))
            hb5 = S5.enter_context(nc.sbuf_tensor("s5_hb", [128, 512], F32))
            hini5 = S5.enter_context(nc.sbuf_tensor("s5_hini", [128, 512], F32))
            hend5 = S5.enter_context(nc.sbuf_tensor("s5_hend", [128, 512], F32))
            Ae5 = S5.enter_context(nc.sbuf_tensor("s5_Ae", [128, 512], F32))
            An5 = S5.enter_context(nc.sbuf_tensor("s5_An", [128, 512], F32))
            A5 = S5.enter_context(nc.sbuf_tensor("s5_A", [128, 512], F32))
            D5 = S5.enter_context(nc.sbuf_tensor("s5_D", [128, 32], F32))
            brow5 = S5.enter_context(nc.sbuf_tensor("s5_brow", [1, TSC], F32))
            bb_ps = S5.enter_context(nc.psum_tensor("s5_bbps", [128, TSC], F32))
            dtgs = [dtg0, dtg1]
            xcgs = [xcg0, xcg1]
            acols5 = [a50, a51]
            ys5 = [y50, y51]

            @block.sync
            def _(s):
                ins = s.dma_start(out=A5[:, :], in_=A_in[:, :])
                c_q.inc(ins, 16)
                ins = s.dma_start(out=D5[:, :], in_=D_in[:, :])
                c_q.inc(ins, 16)
            ad_ld = c_q.v

            @block.gpsimd
            def _(g):
                ins = g.memset(hb5[:, :], 0.0)
                c_aux.inc(ins)

            @block.scalar
            def _(sc):
                sc.wait_ge(c_q.h, ad_ld)
                ins = sc.activation(Ae5[:, :], A5[:, :], AF.Exp)
                c_aux.inc(ins)
                c_aux.wait(sc)
                ins = sc.activation(An5[:, :], Ae5[:, :], AF.Copy, scale=-1.0)
                c_aux.inc(ins)
            an_rdy = c_aux.v

            bcast_marks = {}
            pe_marks5 = {}
            for si in range(32):
                @block.sync
                def _(s, si=si):
                    if si >= 1:
                        s.wait_ge(s_pe.h, pe_marks5[si - 1])
                    ins = s.dma_start(out=brow5[0:1, :], in_=bcd[si:si + 1, :])
                    c_q.inc(ins, 16)
                bcast_marks[si] = c_q.v

                @block.tensor
                def _(t, si=si):
                    t.wait_ge(c_q.h, bcast_marks[si])
                    if si >= 1:
                        c_aux.wait(t, an_rdy + si)
                    last = None
                    for nk in range(4):
                        last = t.matmul(bb_ps[:, nk * 512:(nk + 1) * 512], ones_row[:],
                                        brow5[0:1, nk * 512:(nk + 1) * 512],
                                        start=True, stop=True)
                    s_pe.inc(last)
                pe_marks5[si] = s_pe.v

                @block.vector
                def _(v, si=si):
                    s_pe.wait(v)
                    tgt = Bb5 if si < 16 else Cb5
                    ins = v.tensor_copy(tgt[:, (si % 16) * TSC:((si % 16) + 1) * TSC], bb_ps[:, :])
                    c_aux.inc(ins)
            bc_rdy = c_aux.v

            ldm5 = {}
            outm5 = {}

            @block.vector
            def _(v):
                r = v.alloc_register()
                v.reg_load(r, thrs[0:1, 0:1])
                v.wait_ge(c_hld.h, r)
                v.wait_ge(c_aux.h, bc_rdy)
                ins = v.tensor_scalar(hini5[:, :], hb5[:, :], maskh[:, 0:1], None, ALU.mult)
                c_dv.inc(ins)

            for g_ in range(32):
                kbase = g_ * 16
                slot = g_ % 2

                @block.sync
                def _(s, g_=g_, slot=slot):
                    if g_ >= 2:
                        s.wait_ge(c_a.h, (g_ - 2) * 16 + 16)
                        s.wait_ge(c_dv.h, 1 + (g_ - 2) * 16 + 16)
                    ins = s.dma_start(out=dtgs[slot][:, :],
                                      in_=dtT_d[g_ * 128:(g_ + 1) * 128, :])
                    c_q.inc(ins, 16)
                    ins = s.dma_start(out=xcgs[slot][:, :],
                                      in_=xcT_d[g_ * 128:(g_ + 1) * 128, :])
                    c_q.inc(ins, 16)
                ldm5[g_] = c_q.v

                @block.vector
                def _(v, g_=g_, kbase=kbase, slot=slot):
                    v.wait_ge(c_q.h, ldm5[g_])
                    v.tensor_tensor(dug5[:, :], dtgs[slot][:, :], xcgs[slot][:, :], ALU.mult)
                    if g_ >= 2:
                        v.wait_ge(c_o.h, outm5[g_ - 2])
                    for si in range(16):
                        k = kbase + si
                        v.wait_ge(c_a.h, k + 1)
                        v.tensor_tensor(w5[:, :], dug5[:, :],
                                        Bb5[:, si * TSC:(si + 1) * TSC], ALU.mult)
                        v.tensor_tensor_scan(h5[:, :], acols5[k % 2][:, :], w5[:, :],
                                             hini5[:, k:k + 1], ALU.mult, ALU.add)
                        v.tensor_tensor(hc5[:, :], h5[:, :],
                                        Cb5[:, si * TSC:(si + 1) * TSC], ALU.mult)
                        if k != 511:
                            v.tensor_copy(hend5[:, k:k + 1], h5[:, TSC - 1:TSC])
                        if si == 0:
                            ins = v.scalar_tensor_tensor(ys5[g_ % 2][:, :], xcgs[slot][:, :],
                                                         D5[:, g_:g_ + 1], hc5[:, :],
                                                         ALU.mult, ALU.add)
                        else:
                            ins = v.tensor_tensor(ys5[g_ % 2][:, :], ys5[g_ % 2][:, :],
                                                  hc5[:, :], ALU.add)
                        c_dv.inc(ins)
                        if k == 511:
                            ins = v.tensor_copy(hend5[:, k:k + 1], h5[:, TSC - 1:TSC])
                            c_hend.inc(ins)

                @block.sync
                def _(s, g_=g_):
                    s.wait_ge(c_dv.h, 1 + (g_ + 1) * 16)
                    ins = s.dma_start(out=ysT_d[g_ * 128:(g_ + 1) * 128, :],
                                      in_=ys5[g_ % 2][:, :])
                    c_o.inc(ins, 16)
                outm5[g_] = c_o.v

            @block.scalar
            def _(sc):
                for g_ in range(32):
                    for si in range(16):
                        k = g_ * 16 + si
                        if si == 0:
                            sc.wait_ge(c_q.h, ldm5[g_])
                        if k == 0:
                            sc.wait_ge(c_aux.h, an_rdy)
                        if k >= 2:
                            sc.wait_ge(c_dv.h, 1 + (k - 1))
                        ins = sc.activation(acols5[k % 2][:, :], dtgs[g_ % 2][:, :],
                                            AF.Exp, scale=An5[:, k:k + 1])
                        c_a.inc(ins)

            @block.gpsimd
            def _(g):
                r = g.alloc_register()
                g.reg_load(r, thrs[0:1, 1:2])
                g.wait_ge(c_hend.h, r)
                ins = g.dma_start(out=hst_loc[:, :], in_=hend5[:, :])
                c_g.inc(ins, 16)
                g.wait_ge(c_g.h, c_g.v)
                ins = g.collective_compute("AllGather", ALU.bypass, replica_groups=PAIRS,
                                           ins=[hst_loc.opt()], outs=[hst_ag.opt()])
                s_cc.inc(ins)
                g.wait_ge(s_cc.h, s_cc.v)
                ins = g.dma_start(out=hb5[:, :], in_=hst_ag[0:128, :])
                c_hld.inc(ins, 16)
        s5_out = c_o.v
        s5_cc = s_cc.v

        stage_barrier([(c_o.h, s5_out)])
        w_mm = Sem(nc, "w_mm")
        w_ev = Sem(nc, "w_ev")
        w_sil = Sem(nc, "w_sil")
        w_yg = Sem(nc, "w_yg")
        w_q = Sem(nc, "w_q")
        w_g = Sem(nc, "w_g")
        w_o = Sem(nc, "w_o")

        with ExitStack() as S6:
            x2T6 = S6.enter_context(nc.sbuf_tensor("s6_x2", [128, NT * 1024], BF16))
            ygc = S6.enter_context(nc.sbuf_tensor("s6_yg", [128, 32 * 1024], BF16))
            yb6 = S6.enter_context(nc.sbuf_tensor("s6_yb", [128, NT * 1024], BF16))
            x3c = S6.enter_context(nc.sbuf_tensor("s6_x3c", [128, NT * 512], BF16))
            wb60 = S6.enter_context(nc.sbuf_tensor("s6_wb0", [128, 32 * 128], BF16))
            wb61 = S6.enter_context(nc.sbuf_tensor("s6_wb1", [128, 32 * 128], BF16))
            ld60 = S6.enter_context(nc.sbuf_tensor("s6_ld0", [128, 2048], BF16))
            ld61 = S6.enter_context(nc.sbuf_tensor("s6_ld1", [128, 2048], BF16))
            zs6 = S6.enter_context(nc.sbuf_tensor("s6_zs", [128, 1024], BF16))
            mm6 = S6.enter_context(nc.psum_tensor("s6_mm", [128, 1024], F32))
            wbs6 = [wb60, wb61]
            lds6 = [ld60, ld61]

            stm6 = {}
            for dt_ in range(32):
                slot = dt_ % 2

                @block.gpsimd
                def _(g, dt_=dt_, slot=slot):
                    if dt_ >= 2:
                        g.wait_ge(w_q.h, stm6[dt_ - 2])
                    ins = g.dma_start(out=lds6[slot][:, :], in_=W_out[dt_ * 128:(dt_ + 1) * 128, :])
                    w_g.inc(ins, 16)
                gmark6 = w_g.v

                @block.sync
                def _(s, dt_=dt_, slot=slot, gmark6=gmark6):
                    s.wait_ge(w_g.h, gmark6)
                    ins = s.dma_start(out=wob[:, dt_, :, :].rearrange("t p q -> p t q"),
                                      in_=lds6[slot][:, :].rearrange("p (t q) -> p t q", t=NT))
                    w_q.inc(ins, 16)
                stm6[dt_] = w_q.v
            pre_q6 = w_q.v

            ygm6 = {}
            wm6 = {}
            x2m6 = {}
            for cp in range(2):
                c0 = cp * 1024

                @block.sync
                def _(s, cp=cp, c0=c0):
                    if cp > 0:
                        w_ev.wait(s)
                    ins = s.dma_start(
                        out=x2T6[:, :].rearrange("p (g c) -> p g c", g=NT),
                        in_=x2T_d[:, c0:c0 + 1024].rearrange("(g p) c -> p g c", p=128))
                    w_q.inc(ins, 16)
                x2m6[cp] = w_q.v

                for dt_ in range(32):
                    gi = cp * 32 + dt_
                    slot = gi % 2

                    @block.gpsimd
                    def _(g, gi=gi, dt_=dt_, c0=c0, slot=slot):
                        if gi < 2:
                            g.wait_ge(w_q.h, pre_q6)
                        if gi >= 2:
                            w_yg.wait(g, gi - 1)
                        ins = g.dma_start(out=lds6[slot][:, 0:1024],
                                          in_=ysT_d[dt_ * 128:(dt_ + 1) * 128, c0:c0 + 1024])
                        w_g.inc(ins, 16)
                        ins = g.dma_start(out=lds6[slot][:, 1024:2048],
                                          in_=zT_d[dt_ * 128:(dt_ + 1) * 128, c0:c0 + 1024])
                        w_g.inc(ins, 16)
                    ygm6[gi] = w_g.v

                    @block.scalar
                    def _(sc, gi=gi, slot=slot):
                        sc.wait_ge(w_g.h, ygm6[gi])
                        if gi >= 1:
                            w_yg.wait(sc, gi)
                        ins = sc.activation(zs6[:, :], lds6[slot][:, 1024:2048], AF.Silu)
                        w_sil.inc(ins)

                    @block.vector
                    def _(v, gi=gi, dt_=dt_, slot=slot):
                        w_sil.wait(v)
                        ins = v.tensor_tensor(ygc[:, dt_ * 1024:(dt_ + 1) * 1024],
                                              lds6[slot][:, 0:1024], zs6[:, :], ALU.mult)
                        w_yg.inc(ins)
                yg_done6 = w_yg.v

                for it in range(NT):
                    mi = cp * NT + it
                    slot = mi % 2

                    @block.sync
                    def _(s, mi=mi, it=it, slot=slot):
                        if mi == 0:
                            s.wait_ge(w_q.h, pre_q6)
                        if mi >= 2:
                            w_mm.wait(s, mi - 1)
                        ins = s.dma_start(out=wbs6[slot][:, :].rearrange("p (t q) -> p t q", t=32),
                                          in_=wob[it, :, :, :].rearrange("t p q -> p t q"))
                        w_q.inc(ins, 16)
                    wm6[mi] = w_q.v

                    @block.tensor
                    def _(t, mi=mi, it=it, slot=slot, cp=cp):
                        t.wait_ge(w_q.h, wm6[mi])
                        w_yg.wait(t, yg_done6)
                        if mi > 0:
                            w_ev.wait(t, mi)
                        last = None
                        for dt_ in range(32):
                            for nk in range(2):
                                last = t.matmul(mm6[:, nk * 512:(nk + 1) * 512],
                                                wbs6[slot][:, dt_ * 128:(dt_ + 1) * 128],
                                                ygc[:, dt_ * 1024 + nk * 512: dt_ * 1024 + nk * 512 + 512],
                                                start=(dt_ == 0), stop=(dt_ == 31))
                        w_mm.inc(last)

                    @block.vector
                    def _(v, mi=mi, it=it, cp=cp):
                        w_mm.wait(v)
                        if it == 0:
                            v.wait_ge(w_q.h, x2m6[cp])
                        ins = v.tensor_tensor(yb6[:, it * 1024:(it + 1) * 1024], mm6[:, :],
                                              x2T6[:, it * 1024:(it + 1) * 1024], ALU.add)
                        w_ev.inc(ins)

                for sub in range(2):
                    cc0 = sub * 512
                    ev_now6 = w_ev.v
                    with ExitStack() as S6c:
                        alloc_ln_smalls(S6c, f"_s6_{cp}_{sub}")
                        st6a = S6c.enter_context(nc.psum_tensor(f"s6_sts_{cp}_{sub}", [1, 512], F32))
                        st6b = S6c.enter_context(nc.psum_tensor(f"s6_stq_{cp}_{sub}", [1, 512], F32))
                        muB6 = S6c.enter_context(nc.psum_tensor(f"s6_muB_{cp}_{sub}", [128, 512], F32))
                        ivB6 = S6c.enter_context(nc.psum_tensor(f"s6_ivB_{cp}_{sub}", [128, 512], F32))
                        emit_ln_chunk(
                            (lambda it, cc0=cc0: yb6[:, it * 1024 + cc0: it * 1024 + cc0 + 512]),
                            (lambda it: x3c[:, it * 512:(it + 1) * 512]),
                            st6a, st6b, muB6, ivB6,
                            pre_wait=(lambda sc, ev_now6=ev_now6: sc.wait_ge(w_ev.h, ev_now6)),
                            out_gate=(lambda v: v.wait_ge(w_o.h, w_o.v)))

                        @block.sync
                        def _(s, cp=cp, cc0=cc0, c0=c0):
                            s_nrm.wait(s)
                            ins = s.dma_start(
                                out=x3T_d[:, c0 + cc0: c0 + cc0 + 512]
                                    .rearrange("(g p) c -> p g c", p=128),
                                in_=x3c[:, :].rearrange("p (g c) -> p g c", g=NT))
                            w_o.inc(ins, 16)
        s6_out = w_o.v

        if upto <= 6:
            dx3 = nc.dram_tensor("dbg_x3T", [D, CL], BF16, kind="ExternalOutput").ap()

            @block.gpsimd
            def _(g):
                g.wait_ge(w_o.h, s6_out)
                ins = g.dma_start(out=dx3[:, :], in_=x3T_d[:, :])
                q_o.inc(ins, 16)
                g.wait_ge(q_o.h, q_o.v)
            return nc

        stage_barrier([(w_o.h, s6_out)])
        l_mm = Sem(nc, "l_mm")
        l_gel = Sem(nc, "l_gel")
        l_hid = Sem(nc, "l_hid")
        l_dn = Sem(nc, "l_dn")
        l_y = Sem(nc, "l_y")
        l_q = Sem(nc, "l_q")
        l_g = Sem(nc, "l_g")
        l_o = Sem(nc, "l_o")

        with ExitStack() as S7:
            x3c7 = S7.enter_context(nc.sbuf_tensor("s7_x3c", [128, NT * 1024], BF16))
            hidg = S7.enter_context(nc.sbuf_tensor("s7_hid", [128, NT * 1024], BF16))
            outac = S7.enter_context(nc.sbuf_tensor("s7_oa", [128, NT * 1024], BF16))
            x4c = S7.enter_context(nc.sbuf_tensor("s7_x4c", [128, NT * 512], F32))
            gb70 = S7.enter_context(nc.sbuf_tensor("s7_gb0", [128, NT * 128], BF16))
            gb71 = S7.enter_context(nc.sbuf_tensor("s7_gb1", [128, NT * 128], BF16))
            ub70 = S7.enter_context(nc.sbuf_tensor("s7_ub0", [128, NT * 128], BF16))
            ub71 = S7.enter_context(nc.sbuf_tensor("s7_ub1", [128, NT * 128], BF16))
            db70 = S7.enter_context(nc.sbuf_tensor("s7_db0", [128, NT * 128], BF16))
            db71 = S7.enter_context(nc.sbuf_tensor("s7_db1", [128, NT * 128], BF16))
            gsb = S7.enter_context(nc.sbuf_tensor("s7_gsb", [128, 1024], BF16))
            tsq7 = S7.enter_context(nc.sbuf_tensor("s7_tsq", [128, 1024], BF16))
            vv7 = S7.enter_context(nc.sbuf_tensor("s7_vv", [128, 1024], BF16))
            ww7 = S7.enter_context(nc.sbuf_tensor("s7_ww", [128, 1024], BF16))
            th7 = S7.enter_context(nc.sbuf_tensor("s7_th", [128, 1024], BF16))
            pp7 = S7.enter_context(nc.sbuf_tensor("s7_pp", [128, 1024], BF16))
            gbs7 = [gb70, gb71]
            ubs7 = [ub70, ub71]
            dbs7 = [db70, db71]

            x3m7 = {}
            upm7 = {}
            dnm7 = {}
            for cp in range(2):
                c0 = cp * 1024

                @block.sync
                def _(s, cp=cp, c0=c0):
                    if cp > 0:
                        l_y.wait(s)
                    ins = s.dma_start(
                        out=x3c7[:, :].rearrange("p (g c) -> p g c", g=NT),
                        in_=x3T_d[:, c0:c0 + 1024].rearrange("(g p) c -> p g c", p=128))
                    l_q.inc(ins, 16)
                x3m7[cp] = l_q.v

                with ExitStack() as S7m:
                    U7 = S7m.enter_context(nc.psum_tensor(f"s7_U_{cp}", [128, 1024], F32))
                    G7 = S7m.enter_context(nc.psum_tensor(f"s7_G_{cp}", [128, 1024], F32))
                    D7 = S7m.enter_context(nc.psum_tensor(f"s7_D_{cp}", [128, 1024], F32))

                    for ng in range(8):
                        for ntl in range(NT):
                            nt = ng * NT + ntl
                            ei = cp * 128 + nt
                            slot = ei % 2

                            @block.sync
                            def _(s, ei=ei, nt=nt, slot=slot):
                                if ei == 0:
                                    s.wait_ge(s_cc.h, deq_cc)
                                if ei >= 2:
                                    l_hid.wait(s, ei - 1)
                                ins = s.dma_start(
                                    out=gbs7[slot][:, :].rearrange("p (t q) -> p t q", t=NT),
                                    in_=wgq_ag[nt // 16, :, nt % 16, :, :]
                                        .rearrange("t p q -> p t q"))
                                l_q.inc(ins, 16)
                                ins = s.dma_start(
                                    out=ubs7[slot][:, :].rearrange("p (t q) -> p t q", t=NT),
                                    in_=wuq_ag[nt // 16, :, nt % 16, :, :]
                                        .rearrange("t p q -> p t q"))
                                l_q.inc(ins, 16)
                            upm7[ei] = l_q.v

                            @block.tensor
                            def _(t, ei=ei, slot=slot, cp=cp):
                                t.wait_ge(l_q.h, upm7[ei])
                                if ei % 128 == 0:
                                    t.wait_ge(l_q.h, x3m7[cp])
                                if ei > 0:
                                    l_hid.wait(t, ei)
                                last = None
                                for ht in range(NT):
                                    for nk in range(2):
                                        t.matmul(G7[:, nk * 512:(nk + 1) * 512],
                                                 gbs7[slot][:, ht * 128:(ht + 1) * 128],
                                                 x3c7[:, ht * 1024 + nk * 512: ht * 1024 + nk * 512 + 512],
                                                 start=(ht == 0), stop=(ht == NT - 1))
                                        last = t.matmul(U7[:, nk * 512:(nk + 1) * 512],
                                                        ubs7[slot][:, ht * 128:(ht + 1) * 128],
                                                        x3c7[:, ht * 1024 + nk * 512: ht * 1024 + nk * 512 + 512],
                                                        start=(ht == 0), stop=(ht == NT - 1))
                                l_mm.inc(last)

                            @block.vector
                            def _(v, ei=ei):
                                l_mm.wait(v)
                                ins = v.tensor_copy(gsb[:, :], G7[:, :])
                                l_gel.inc(ins)

                            @block.scalar
                            def _(sc, ei=ei):
                                l_gel.wait(sc)
                                ins = sc.activation(tsq7[:, :], gsb[:, :], AF.Square)
                                l_gel.inc(ins)

                            @block.vector
                            def _(v, ei=ei):
                                l_gel.wait(v)
                                v.tensor_scalar(vv7[:, :], tsq7[:, :], 0.044715, 1.0,
                                                ALU.mult, ALU.add)
                                ins = v.tensor_tensor(ww7[:, :], vv7[:, :], gsb[:, :], ALU.mult)
                                l_gel.inc(ins)

                            @block.scalar
                            def _(sc, ei=ei):
                                l_gel.wait(sc)
                                ins = sc.activation(th7[:, :], ww7[:, :], AF.Tanh,
                                                    scale=0.7978845608028654)
                                l_gel.inc(ins)

                            @block.vector
                            def _(v, ei=ei, ntl=ntl):
                                l_gel.wait(v)
                                v.scalar_tensor_tensor(pp7[:, :], th7[:, :], 1.0, gsb[:, :],
                                                       ALU.add, ALU.mult)
                                ins = v.scalar_tensor_tensor(hidg[:, ntl * 1024:(ntl + 1) * 1024],
                                                             pp7[:, :], 0.5, U7[:, :],
                                                             ALU.mult, ALU.mult)
                                l_hid.inc(ins)

                        for h2t in range(NT):
                            di = (cp * 8 + ng) * NT + h2t
                            slot = di % 2

                            @block.gpsimd
                            def _(g, di=di, ng=ng, h2t=h2t, slot=slot):
                                if di >= 2:
                                    l_dn.wait(g, di - 1)
                                ins = g.dma_start(
                                    out=dbs7[slot][:, :].rearrange("p (t q) -> p t q", t=NT),
                                    in_=wdq_ag[h2t // 2, ng * NT:(ng + 1) * NT, h2t % 2, :, :]
                                        .rearrange("t p q -> p t q"))
                                l_g.inc(ins, 16)
                            dnm7[di] = l_g.v

                            @block.tensor
                            def _(t, di=di, cp=cp, ng=ng, slot=slot):
                                t.wait_ge(l_g.h, dnm7[di])
                                l_hid.wait(t, cp * 128 + (ng + 1) * NT)
                                if di > 0:
                                    l_y.wait(t, di)
                                last = None
                                for ntl in range(NT):
                                    for nk in range(2):
                                        last = t.matmul(D7[:, nk * 512:(nk + 1) * 512],
                                                        dbs7[slot][:, ntl * 128:(ntl + 1) * 128],
                                                        hidg[:, ntl * 1024 + nk * 512: ntl * 1024 + nk * 512 + 512],
                                                        start=(ntl == 0), stop=(ntl == NT - 1))
                                l_dn.inc(last)

                            @block.vector
                            def _(v, di=di, ng=ng, h2t=h2t):
                                l_dn.wait(v)
                                if ng == 0:
                                    ins = v.tensor_tensor(outac[:, h2t * 1024:(h2t + 1) * 1024],
                                                          D7[:, :],
                                                          x3c7[:, h2t * 1024:(h2t + 1) * 1024],
                                                          ALU.add)
                                else:
                                    ins = v.tensor_tensor(outac[:, h2t * 1024:(h2t + 1) * 1024],
                                                          outac[:, h2t * 1024:(h2t + 1) * 1024],
                                                          D7[:, :], ALU.add)
                                l_y.inc(ins)
                    y_done7 = l_y.v

                for sub in range(2):
                    cc0 = sub * 512
                    with ExitStack() as S7c:
                        alloc_ln_smalls(S7c, f"_s7_{cp}_{sub}")
                        st7a = S7c.enter_context(nc.psum_tensor(f"s7_sts_{cp}_{sub}", [1, 512], F32))
                        st7b = S7c.enter_context(nc.psum_tensor(f"s7_stq_{cp}_{sub}", [1, 512], F32))
                        muB7 = S7c.enter_context(nc.psum_tensor(f"s7_muB_{cp}_{sub}", [128, 512], F32))
                        ivB7 = S7c.enter_context(nc.psum_tensor(f"s7_ivB_{cp}_{sub}", [128, 512], F32))
                        emit_ln_chunk(
                            (lambda it, cc0=cc0: outac[:, it * 1024 + cc0: it * 1024 + cc0 + 512]),
                            (lambda it: x4c[:, it * 512:(it + 1) * 512]),
                            st7a, st7b, muB7, ivB7,
                            pre_wait=(lambda sc, y_done7=y_done7: sc.wait_ge(l_y.h, y_done7)),
                            out_gate=(lambda v: v.wait_ge(l_o.h, l_o.v)))

                        @block.sync
                        def _(s, cp=cp, cc0=cc0, c0=c0):
                            s_nrm.wait(s)
                            ins = s.dma_start(
                                out=out_x4[:, c0 + cc0: c0 + cc0 + 512]
                                    .rearrange("(g p) c -> p g c", p=128),
                                in_=x4c[:, :].rearrange("p (g c) -> p g c", g=NT))
                            l_o.inc(ins, 16)

        @block.sync
        def _(s):
            s.wait_ge(l_o.h, l_o.v)
            s.wait_ge(s_cc.h, s_cc.v)
        return nc

        if upto <= 5:
            dys = nc.dram_tensor("dbg_ysT", [DI, CL], BF16, kind="ExternalOutput").ap()

            @block.scalar
            def _(sc):
                pass

            @block.gpsimd
            def _(g):
                g.wait_ge(c_o.h, s5_out)
                ins = g.dma_start(out=dys[:, :], in_=ysT_d[:, :])
                q_o.inc(ins, 16)
                g.wait_ge(q_o.h, q_o.v)
            return nc

        if upto <= 4:
            dxc = nc.dram_tensor("dbg_xcT", [DI, CL], BF16, kind="ExternalOutput").ap()
            dz = nc.dram_tensor("dbg_zT", [DI, CL], BF16, kind="ExternalOutput").ap()
            ddt = nc.dram_tensor("dbg_dtT", [DI, CL], BF16, kind="ExternalOutput").ap()
            dbc = nc.dram_tensor("dbg_bcd", [32, CL], F32, kind="ExternalOutput").ap()

            @block.gpsimd
            def _(g):
                g.wait_ge(p_o.h, s4_out)
                for dst, src in ((dxc, xcT_d), (dz, zT_d), (ddt, dtT_d), (dbc, bcd)):
                    ins = g.dma_start(out=dst[:, :], in_=src[:, :])
                    q_o.inc(ins, 16)
                g.wait_ge(q_o.h, q_o.v)
                s_cc.wait(g)
            return nc

        if upto <= 3:
            dbgx2 = nc.dram_tensor("dbg_x2T", [D, CL], BF16, kind="ExternalOutput").ap()
            dbghal = nc.dram_tensor("dbg_hal", [4, D], BF16, kind="ExternalOutput").ap()

            @block.gpsimd
            def _(g):
                g.wait_ge(f_o.h, s3_stores)
                s_cc.wait(g)
                ins = g.dma_start(out=dbgx2[:, :], in_=x2T_d[:, :])
                q_o.inc(ins, 16)
                ins = g.dma_start(out=dbghal[:, :], in_=hal_ag[:, :])
                q_o.inc(ins, 16)
                g.wait_ge(q_o.h, q_o.v)
            return nc

        if upto <= 2:
            dbgx1 = nc.dram_tensor("dbg_x1T", [D, CL], BF16, kind="ExternalOutput").ap()

            @block.gpsimd
            def _(g):
                g.wait_ge(q_o.h, s2_stores)
                ins = g.dma_start(out=dbgx1[:, :], in_=x1T_d[:, :])
                q_o.inc(ins, 16)
                g.wait_ge(q_o.h, q_o.v)
                s_cc.wait(g)
            return nc
        return nc


import ml_dtypes as _mld
from concourse.bass_utils import run_bass_kernel_spmd as _run

_BF = _mld.bfloat16
_CACHE = {}


def _shard_all(I):
    eye = np.eye(128, dtype=_BF)
    x = I["L0_attn_x"]
    conv_w = I["L1_attn_conv_w"][:, 0, :]
    conv_w_r = np.ascontiguousarray(conv_w.reshape(32, 128, 3).transpose(1, 0, 2).reshape(128, 96))
    conv_b_r = np.ascontiguousarray(I["L1_attn_conv_b"].reshape(32, 128).T)
    b_dt_r = np.ascontiguousarray(I["L1_attn_b_dt"].reshape(32, 128).T)
    A_r = np.ascontiguousarray(
        I["L1_attn_A_log"].reshape(32, 128, 16).transpose(1, 0, 2).reshape(128, 512))
    D_r = np.ascontiguousarray(I["L1_attn_D"].reshape(32, 128).T)
    maps = []
    for k in range(8):
        b, h = k // 2, k % 2
        ns = slice(k * 2048, (k + 1) * 2048)
        hs = slice(k * 256, (k + 1) * 256)
        es = slice(k * 1024, (k + 1) * 1024)
        ds = slice(k * 256, (k + 1) * 256)
        maskh = np.full((128, 1), 1.0 if h == 1 else 0.0, np.float32)
        maskl = np.zeros((128, 2), np.float32)
        maskl[:, 0] = 1.0 if h == 1 else 0.0
        maskl[:, 1] = 1.0 if h == 0 else 0.0
        thrv = np.array([[16 if h == 1 else 0, 1 if h == 0 else 0]], np.int32)
        maps.append({
            "x_own": np.ascontiguousarray(x[b, h * 2048:(h + 1) * 2048, :]),
            "x_sib": np.ascontiguousarray(x[b, (1 - h) * 2048:(2 - h) * 2048, :]),
            "W_up_s": np.ascontiguousarray(I["L0_ffn_W_up"][es]),
            "W_gate_s": np.ascontiguousarray(I["L0_ffn_W_gate"][es]),
            "W_down_s": np.ascontiguousarray(I["L0_ffn_W_down"][ds]),
            "W_in": I["L1_attn_W_in"], "conv_w_r": conv_w_r, "conv_b_r": conv_b_r,
            "W_x": I["L1_attn_W_x"], "W_dt": I["L1_attn_W_dt"], "b_dt_r": b_dt_r,
            "A_r": A_r, "D_r": D_r, "W_out": I["L1_attn_W_out"],
            "gp_pk": np.ascontiguousarray(I["L1_ffn_gate_W_packed"][ns]),
            "gp_sc": np.ascontiguousarray(I["L1_ffn_gate_scales"][ns]),
            "up_pk": np.ascontiguousarray(I["L1_ffn_up_W_packed"][ns]),
            "up_sc": np.ascontiguousarray(I["L1_ffn_up_scales"][ns]),
            "dn_pk": np.ascontiguousarray(I["L1_ffn_down_W_packed"][hs]),
            "dn_sc": np.ascontiguousarray(I["L1_ffn_down_scales"][hs]),
            "eye_bf": eye, "maskh": maskh, "maskl": maskl, "thrv": thrv,
        })
    return maps


def kernel(**inputs):
    if "nc" not in _CACHE:
        _CACHE["nc"] = build_fused()
    maps = _shard_all(inputs)
    res = _run(_CACHE["nc"], maps, core_ids=list(range(8)))
    out = np.zeros((4, 4096, 2048), np.float32)
    for k in range(8):
        b, h = k // 2, k % 2
        out[b, h * 2048:(h + 1) * 2048, :] = res.results[k]["x4T"].T
    return out
